# version 02
# baseline (speedup 1.0000x reference)
import numpy as np
import ml_dtypes
from contextlib import ExitStack
import concourse.bass as bass
import concourse.mybir as mybir
from concourse.bass_utils import run_bass_kernel_spmd

F32 = mybir.dt.float32
BF16 = mybir.dt.bfloat16
I32 = mybir.dt.int32
AX = mybir.AxisListType
ALU = mybir.AluOpType
ACT = mybir.ActivationFunctionType

B, T, D = 2, 16384, 1024
H = 4 * D
TSEG = T // 4
C = 128
G = 4
GT = G * C
NG = TSEG // GT
ND = D // 128
NH = H // 128
NT = TSEG // C
REPLICA_GROUPS = [[0, 1, 2, 3], [4, 5, 6, 7]]

ENG_ATTR = {"pe": "tensor", "dve": "vector", "act": "scalar", "sp": "sync", "gp": "gpsimd"}


class _SerialEng:

    _PASSTHRU = {"wait_ge"}

    def __init__(self, eng, sem, state):
        self._eng = eng
        self._sem = sem
        self._state = state

    def __getattr__(self, name):
        fn = getattr(self._eng, name)
        if name in self._PASSTHRU or not callable(fn):
            return fn

        def wrapped(*a, **kw):
            st = self._state
            if st["issued"] > st["waited"]:
                self._eng.wait_ge(self._sem, st["issued"])
                st["waited"] = st["issued"]
            inst = fn(*a, **kw)
            inst.then_inc(self._sem, 1)
            st["issued"] += 1
            return inst
        return wrapped


class Sched:
    ENGS = ("pe", "dve", "act", "sp", "gp")
    SERIAL = {"dve", "act"}

    def __init__(self, nc, block, es):
        self.nc = nc
        self.block = block
        self.sem = {e: es.enter_context(nc.semaphore("sem_" + e)) for e in self.ENGS}
        self.sem["cc"] = es.enter_context(nc.semaphore("sem_cc"))
        self.cnt = {e: 0 for e in self.ENGS}
        self.cnt["cc"] = 0
        self.waited = {e: {e2: 0 for e2 in (*self.ENGS, "cc")} for e in self.ENGS}
        self.serial_state = {e: {"issued": 0, "waited": 0} for e in self.SERIAL}

    def step(self, eng, deps, body, n_dma=0):
        sem = self.sem
        waited = self.waited[eng]
        serial = eng in self.SERIAL
        use = []
        for d in deps:
            if d is None:
                continue
            de, dv = d
            if de == eng:
                if serial:
                    continue
            if dv > waited[de]:
                if de == eng and not n_dma and not serial:
                    continue
                use.append((de, dv))
                waited[de] = dv
        n_body_inst = [0]
        own_wait = 0
        if n_dma:
            own_wait = self.cnt[eng]
            if own_wait <= waited[eng]:
                own_wait = 0
            else:
                waited[eng] = own_wait
            self.cnt[eng] += 16 * n_dma
        issued = [0]

        def emit(e):
            if own_wait:
                e.wait_ge(sem[eng], own_wait)
            for de, dv in use:
                e.wait_ge(sem[de], dv)
            if n_dma:
                def dma(out, in_, **kw):
                    issued[0] += 1
                    return e.dma_start(out=out, in_=in_, **kw).then_inc(sem[eng], 16)
                body(e, dma)
                assert issued[0] == n_dma, f"declared {n_dma} dmas, issued {issued[0]}"
            elif serial:
                body(_SerialEng(e, sem[eng], self.serial_state[eng]))
            else:
                last = body(e)
                assert last is not None, "compute step must return its last instruction"
                last.then_inc(sem[eng], 1)

        if serial and not n_dma:
            getattr(self.block, ENG_ATTR[eng])(emit)
            self.cnt[eng] = self.serial_state[eng]["issued"]
            return (eng, self.cnt[eng])
        if not n_dma:
            self.cnt[eng] += 1
        getattr(self.block, ENG_ATTR[eng])(emit)
        return (eng, self.cnt[eng])

    def all_done(self):
        return [(e, self.cnt[e]) for e in self.ENGS]


def _finish(S, block):
    sem = S.sem
    cnts = {e: S.cnt[e] for e in S.ENGS}

    def gp_final(e):
        for e2 in S.ENGS:
            if cnts[e2]:
                e.wait_ge(sem[e2], cnts[e2])
    block.gpsimd(gp_final)


def mm_acc(pe, out, pairs):
    n = len(pairs)
    last = None
    for i, (l, r) in enumerate(pairs):
        last = pe.matmul(out, l, r, start=(i == 0), stop=(i == n - 1))
    return last


def hl3(lh, ll, rh, rl):
    return [(lh, rh), (lh, rl), (ll, rh)]


class _StopEmit(Exception):
    pass


def build_program(dbg=False, upto='p3b'):
    p2sub = upto.split('@', 1)[1] if upto.startswith('p2@') else None
    if p2sub:
        upto = 'p2'
    nc = bass.Bass("TRN2", target_bir_lowering=False)

    def dram_in(name, shape, dt=F32):
        return nc.dram_tensor(name, shape, dt, kind="ExternalInput").ap()

    def dram_scratch(name, shape, dt=F32):
        if dbg:
            return nc.dram_tensor(name, shape, dt, kind="ExternalOutput").ap()
        return nc.dram_tensor(name, shape, dt).ap()

    x_d = dram_in("x_seg", [TSEG, D])
    xT_d = dram_in("xT_seg", [D, TSEG])
    WT_d = {w: dram_in(w, [D, D]) for w in ("WqT", "WkT", "WvT", "WoT")}
    bq_d = dram_in("bq_t", [128, ND]); bk_d = dram_in("bk_t", [128, ND])
    bvb_d = dram_in("bv_b", [128, D]); bob_d = dram_in("bo_b", [128, D])
    w1T_d = dram_in("w1T", [D, H]); w2T_d = dram_in("w2T", [H, D])
    triu_d = dram_in("triu", [128, 128])
    ident_d = dram_in("ident", [128, 128])
    pmask_d = dram_in("pmask", [128, 4])
    out_d = nc.dram_tensor("out", [TSEG, D], F32, kind="ExternalOutput").ap()

    spill_k = dram_scratch("spill_k", [NG, 2, 128, G * D], BF16)
    spill_v = dram_scratch("spill_v", [NG, 2, 128, G * D], BF16)
    spill_q = dram_scratch("spill_q", [NG, 2, 128, ND * GT], BF16)
    prefix_s = dram_scratch("prefix_s", [NG, 128, ND * D])
    prefix_z = dram_scratch("prefix_z", [NG, 128, ND])
    y1_dram = dram_scratch("y1_spill", [TSEG, D])
    hq_dram = dram_scratch("hq_sp", [NT, 128, H], BF16)
    w1q_dram = dram_scratch("w1q", [128, ND * H], BF16)
    w2q_dram = dram_scratch("w2q", [128, NH * D], BF16)
    sstart_d = dram_scratch("sstart", [128, ND * D])
    cc_in = nc.dram_tensor("cc_in", [D + 1, D], F32).ap()
    cc_out = nc.dram_tensor("cc_out", [4 * (D + 1), D], F32).ap()

    es = ExitStack()
    with es:
        block = es.enter_context(nc.Block())
        S = Sched(nc, block, es)
        sem = S.sem

        def SB(scope, name, shape, dt=F32):
            return scope.enter_context(nc.sbuf_tensor(name, shape, dt))

        def PS(scope, name, shape, dt=F32):
            return scope.enter_context(nc.psum_tensor(name, shape, dt))

        ones_bf = SB(es, "ones_bf", [128, 128], BF16)
        ident_f = SB(es, "ident_f", [128, 128], F32)
        ident_b = SB(es, "ident_b", [128, 128], BF16)
        triu_f = SB(es, "triu_f", [128, 128], F32)
        s1bc = SB(es, "s1bc", [128, 1]); s2bc = SB(es, "s2bc", [128, 1])
        thr1 = SB(es, "thr1", [128, 1]); nthr1 = SB(es, "nthr1", [128, 1])
        thr2 = SB(es, "thr2", [128, 1]); nthr2 = SB(es, "nthr2", [128, 1])
        bq_s = SB(es, "bq_s", [128, ND]); bk_s = SB(es, "bk_s", [128, ND])
        bvb_s = SB(es, "bvb_s", [128, D]); bob_s = SB(es, "bob_s", [128, D])
        pmask_s = SB(es, "pmask_s", [128, 4])
        z_acc = SB(es, "z_acc", [128, ND]); z_start = SB(es, "z_start", [128, ND])
        zg = SB(es, "zg", [128, ND])
        zg_h = SB(es, "zg_h", [128, ND], BF16); zg_l = SB(es, "zg_l", [128, ND], BF16)
        f2all = SB(es, "f2all", [128, NT])
        eps_c = SB(es, "eps_c", [128, 1])
        s_scope = ExitStack()
        Smaster = SB(s_scope, "Smaster", [128, ND * D])

        def ld_const(e, dma):
            dma(out=ident_f[:], in_=ident_d[:])
            dma(out=triu_f[:], in_=triu_d[:])
            dma(out=bq_s[:], in_=bq_d[:])
            dma(out=bk_s[:], in_=bk_d[:])
            dma(out=bvb_s[:], in_=bvb_d[:])
            dma(out=bob_s[:], in_=bob_d[:])
            dma(out=pmask_s[:], in_=pmask_d[:])
        h_const = S.step("sp", [], ld_const, n_dma=7)

        def init_body(v):
            v.memset(ones_bf[:], 1.0)
            v.memset(eps_c[:], 1e-5)
            v.memset(z_acc[:], 0.0)
            v.memset(Smaster[:], 0.0)
            return v.tensor_copy(ident_b[:], ident_f[:])
        h_init = S.step("dve", [h_const], init_body)

        p0 = ExitStack()
        with p0:
            wpiece = SB(p0, "wpiece", [128, H])
            wsum = SB(p0, "wsum", [128, 1]); wtmp1 = SB(p0, "wtmp1", [128, 1])
            sh_ = SB(p0, "sh_", [128, 1], BF16); sl_ = SB(p0, "sl_", [128, 1], BF16)
            sc_row = SB(p0, "sc_row", [1, 2])
            sc_h = SB(p0, "sc_h", [1, 2], BF16); sc_l = SB(p0, "sc_l", [1, 2], BF16)
            cge = SB(p0, "cge", [128, H])
            tq_ = SB(p0, "tq_", [128, H], BF16)
            ps_s = PS(p0, "ps_s", [1, 2])
            ps_b = PS(p0, "ps_b", [128, 2])

            prev_wi_spill = None
            for wi, (wT_dd, npieces, pw, wq_dram, sbc, thr, nthr) in enumerate([
                    (w1T_d, ND, H, w1q_dram, s1bc, thr1, nthr1),
                    (w2T_d, NH, D, w2q_dram, s2bc, thr2, nthr2)]):
                hz = S.step("dve", [h_init], lambda v: v.memset(wsum[:], 0.0))
                hr = hz
                for pi in range(npieces):
                    def ldp(e, dma, pi=pi, wT_dd=wT_dd, pw=pw):
                        dma(out=wpiece[:, :pw], in_=wT_dd[pi * 128:(pi + 1) * 128, :])
                    hl = S.step("sp", [hr], ldp, n_dma=1)

                    def red(v, pw=pw):
                        v.reduce_sum(wtmp1[:], wpiece[:, :pw], axis=AX.X,
                                     apply_absolute_value=True)
                        return v.tensor_add(wsum[:], wsum[:], wtmp1[:])
                    hr = S.step("dve", [hl], red)

                def spl(v):
                    v.tensor_copy(sh_[:], wsum[:])
                    return v.tensor_sub(sl_[:], wsum[:], sh_[:])
                hsp = S.step("dve", [hr], spl)
                hmm = S.step("pe", [hsp], lambda p, wi=wi: mm_acc(
                    p, ps_s[0:1, wi:wi + 1],
                    [(sh_[:], ones_bf[:, 0:1]), (sl_[:], ones_bf[:, 0:1])]))
                nelem = float(npieces * 128 * pw)

                def mean(v, wi=wi, nelem=nelem):
                    v.tensor_scalar(sc_row[0:1, wi:wi + 1], ps_s[0:1, wi:wi + 1],
                                    1.0 / nelem, 1e-5, ALU.mult, ALU.max)
                    v.tensor_copy(sc_h[0:1, wi:wi + 1], sc_row[0:1, wi:wi + 1])
                    return v.tensor_sub(sc_l[0:1, wi:wi + 1], sc_row[0:1, wi:wi + 1],
                                        sc_h[0:1, wi:wi + 1])
                hmean = S.step("dve", [hmm], mean)
                hbc = S.step("pe", [hmean], lambda p, wi=wi: mm_acc(
                    p, ps_b[:, wi:wi + 1],
                    [(ones_bf[0:1, :], sc_h[0:1, wi:wi + 1]),
                     (ones_bf[0:1, :], sc_l[0:1, wi:wi + 1])]))

                def mkthr(v, sbc=sbc, thr=thr, nthr=nthr, wi=wi):
                    v.tensor_copy(sbc[:], ps_b[:, wi:wi + 1])
                    v.tensor_scalar(thr[:], sbc[:], 0.5, None, ALU.mult)
                    return v.tensor_scalar(nthr[:], sbc[:], -0.5, None, ALU.mult)
                hthr = S.step("dve", [hbc], mkthr)

                hq = hthr
                for pi in range(npieces):
                    def ldp2(e, dma, pi=pi, wT_dd=wT_dd, pw=pw):
                        dma(out=wpiece[:, :pw], in_=wT_dd[pi * 128:(pi + 1) * 128, :])
                    hl = S.step("sp", [hq, prev_wi_spill], ldp2, n_dma=1)

                    def qz(v, pw=pw, thr=thr, nthr=nthr):
                        v.tensor_scalar(cge[:, :pw], wpiece[:, :pw], nthr[:, 0:1], None,
                                        ALU.is_le)
                        return v.scalar_tensor_tensor(
                            tq_[:, :pw], wpiece[:, :pw], thr[:, 0:1], cge[:, :pw],
                            ALU.is_ge, ALU.subtract)
                    hcmp = S.step("dve", [hl], qz)

                    def stq(e, dma, pi=pi, wq_dram=wq_dram, pw=pw):
                        dma(out=wq_dram[:, pi * pw:(pi + 1) * pw], in_=tq_[:, :pw])
                    hq = S.step("gp", [hcmp], stq, n_dma=1)
                prev_wi_spill = hq
            h_p0_done = S.all_done()
        if upto == 'p0':
            _finish(S, block); s_scope.close(); return nc

        p1 = ExitStack()
        with p1:
            wk_h = SB(p1, "wk_h", [128, ND * D], BF16); wk_l = SB(p1, "wk_l", [128, ND * D], BF16)
            wv_h = SB(p1, "wv_h", [128, ND * D], BF16); wv_l = SB(p1, "wv_l", [128, ND * D], BF16)
            ws_scope = ExitStack()
            wstage = SB(ws_scope, "wstage", [128, ND * D])

            hws = h_p0_done
            for (wd, wh, wl) in [(WT_d["WkT"], wk_h, wk_l), (WT_d["WvT"], wv_h, wv_l)]:
                def ldw(e, dma, wd=wd):
                    dma(out=wstage[:].rearrange("p (dt o) -> p dt o", dt=ND),
                        in_=wd.rearrange("(dt p) o -> p dt o", p=128))
                hlw = S.step("sp", hws if isinstance(hws, list) else [hws], ldw, n_dma=1)

                def splw(v, wh=wh, wl=wl):
                    v.tensor_copy(wh[:], wstage[:])
                    return v.tensor_sub(wl[:], wstage[:], wh[:])
                hws = S.step("dve", [hlw], splw)
            ws_scope.close()
            h_wkv = hws

            xT_st = SB(p1, "xT_st", [128, ND * GT])
            xh = SB(p1, "xh", [128, ND * GT], BF16); xl = SB(p1, "xl", [128, ND * GT], BF16)
            kD_f = SB(p1, "kD_f", [128, ND * GT])
            kT_f = SB(p1, "kT_f", [128, G * D])
            kT_h = SB(p1, "kT_h", [128, G * D], BF16); kT_l = SB(p1, "kT_l", [128, G * D], BF16)
            vT_f = SB(p1, "vT_f", [128, 512])
            vT_h = SB(p1, "vT_h", [128, G * D], BF16); vT_l = SB(p1, "vT_l", [128, G * D], BF16)
            ztmp = SB(p1, "ztmp", [128, 1])
            pk = PS(p1, "pk", [128, 2 * GT])
            pv = PS(p1, "pv", [128, D])
            pkv = PS(p1, "pkv", [128, 2 * 512])
            ptr1 = PS(p1, "ptr1", [128, 128])

            prev_split = h_wkv
            prev_x_pe = None
            prev_kv_done = h_init
            prev_spk = None; prev_spv = None
            prev_kv_pe = None
            for g in range(NG):
                def ldx(e, dma, g=g):
                    dma(out=xT_st[:].rearrange("p (dt t) -> p dt t", dt=ND),
                        in_=xT_d[:, g * GT:(g + 1) * GT]
                        .rearrange("(dt p) t -> p dt t", p=128))
                hx = S.step("sp", [prev_split], ldx, n_dma=1)

                def splx(v):
                    v.tensor_copy(xh[:], xT_st[:])
                    return v.tensor_sub(xl[:], xT_st[:], xh[:])
                hxs = S.step("dve", [hx, h_wkv, prev_x_pe], splx)

                def spz(e, dma, g=g):
                    dma(out=prefix_z[g], in_=z_acc[:])
                    dma(out=prefix_s[g], in_=Smaster[:])
                hspill = S.step("gp", [prev_kv_done], spz, n_dma=2)

                hk_evac = []
                for ot in range(ND):
                    def mmk(p, ot=ot):
                        pairs = []
                        for dk in range(ND):
                            lh = wk_h[:, dk * D + ot * 128: dk * D + ot * 128 + 128]
                            ll = wk_l[:, dk * D + ot * 128: dk * D + ot * 128 + 128]
                            rh = xh[:, dk * GT:(dk + 1) * GT]
                            rl = xl[:, dk * GT:(dk + 1) * GT]
                            pairs += hl3(lh, ll, rh, rl)
                        return mm_acc(p, pk[:, (ot % 2) * GT:(ot % 2 + 1) * GT], pairs)
                    dep = [hxs] + ([hk_evac[ot - 2][1]] if ot >= 2 else [])
                    hmk = S.step("pe", dep, mmk)

                    def ek_act(a, ot=ot):
                        return a.activation(kT_f[:, 0:GT], pk[:, (ot % 2) * GT:(ot % 2 + 1) * GT],
                                            ACT.Exp, bias=bk_s[:, ot:ot + 1])
                    hek = S.step("act", [hmk, prev_spk, prev_kv_pe]
                                 + ([hk_evac[ot - 1][1]] if ot >= 1 else []),
                                 ek_act)

                    def ek_dve(v, ot=ot):
                        v.tensor_scalar(kT_f[:, GT:2 * GT], pk[:, (ot % 2) * GT:(ot % 2 + 1) * GT],
                                        bk_s[:, ot:ot + 1], 0.0, ALU.add, ALU.max)
                        v.scalar_tensor_tensor(kD_f[:, ot * GT:(ot + 1) * GT],
                                               kT_f[:, 0:GT], 1.0, kT_f[:, GT:2 * GT],
                                               ALU.min, ALU.add)
                        v.reduce_sum(ztmp[:], kD_f[:, ot * GT:(ot + 1) * GT], axis=AX.X)
                        return v.tensor_add(z_acc[:, ot:ot + 1], z_acc[:, ot:ot + 1], ztmp[:])
                    hev = S.step("dve", [hmk, hek, hspill, prev_x_pe, prev_spk], ek_dve)
                    hk_evac.append((hmk, hev))

                htr_prev_ev = None
                for tc in range(G):
                    for dt in range(ND):
                        def trk(p, tc=tc, dt=dt):
                            return p.transpose(
                                ptr1[:],
                                kD_f[:, dt * GT + tc * 128: dt * GT + tc * 128 + 128],
                                ident_f[:])
                        dep = [hk_evac[dt][1]] + ([htr_prev_ev] if htr_prev_ev else [])
                        htk = S.step("pe", dep, trk)

                        def evtr(v, tc=tc, dt=dt):
                            return v.tensor_copy(
                                kT_f[:, tc * D + dt * 128: tc * D + dt * 128 + 128],
                                ptr1[:])
                        htr_prev_ev = S.step("dve", [htk], evtr)

                def splkT(v):
                    v.tensor_copy(kT_h[:], kT_f[:])
                    return v.tensor_sub(kT_l[:], kT_f[:], kT_h[:])
                hkT = S.step("dve", [htr_prev_ev, prev_spk, prev_kv_pe], splkT)

                def spk(e, dma, g=g):
                    dma(out=spill_k[g, 0], in_=kT_h[:])
                    dma(out=spill_k[g, 1], in_=kT_l[:])
                hspk = S.step("gp", [hkT], spk, n_dma=2)

                hv_evac = []
                for tt in range(G):
                    hmv_eh = None
                    for eh in range(2):
                        def mmv(p, tt=tt, eh=eh):
                            pairs = []
                            for dk in range(ND):
                                lh = xh[:, dk * GT + tt * 128: dk * GT + tt * 128 + 128]
                                ll = xl[:, dk * GT + tt * 128: dk * GT + tt * 128 + 128]
                                rh = wv_h[:, dk * D + eh * 512: dk * D + eh * 512 + 512]
                                rl = wv_l[:, dk * D + eh * 512: dk * D + eh * 512 + 512]
                                pairs += hl3(lh, ll, rh, rl)
                            return mm_acc(p, pv[:, eh * 512:(eh + 1) * 512], pairs)
                        dep = [hxs] + ([hv_evac[tt - 1][1]] if tt >= 1 else [])
                        hmv_eh = S.step("pe", dep, mmv)

                    def ev_v(v, tt=tt):
                        last = None
                        for eh in range(2):
                            dst = slice(tt * D + eh * 512, tt * D + eh * 512 + 512)
                            v.scalar_tensor_tensor(vT_f[:], pv[:, eh * 512:(eh + 1) * 512],
                                                   0.0, bvb_s[:, eh * 512:(eh + 1) * 512],
                                                   ALU.add, ALU.add)
                            v.tensor_copy(vT_h[:, dst], vT_f[:])
                            last = v.tensor_sub(vT_l[:, dst], vT_f[:], vT_h[:, dst])
                        return last
                    hve = S.step("dve", [hmv_eh, prev_spv, prev_kv_pe], ev_v)
                    hv_evac.append((hmv_eh, hve))

                def spv(e, dma, g=g):
                    dma(out=spill_v[g, 0], in_=vT_h[:])
                    dma(out=spill_v[g, 1], in_=vT_l[:])
                hspv = S.step("gp", [hv_evac[-1][1]], spv, n_dma=2)

                hkv_ev = []
                hm_last = None
                for dt in range(ND):
                    for eh in range(2):
                        idx = dt * 2 + eh

                        def mmkv(p, dt=dt, eh=eh, idx=idx):
                            pairs = []
                            for tc in range(G):
                                lh = kT_h[:, tc * D + dt * 128: tc * D + dt * 128 + 128]
                                ll = kT_l[:, tc * D + dt * 128: tc * D + dt * 128 + 128]
                                rh = vT_h[:, tc * D + eh * 512: tc * D + eh * 512 + 512]
                                rl = vT_l[:, tc * D + eh * 512: tc * D + eh * 512 + 512]
                                pairs += hl3(lh, ll, rh, rl)
                            slot = idx % 2
                            return mm_acc(p, pkv[:, slot * 512:(slot + 1) * 512], pairs)
                        dep = [hkT, hv_evac[-1][1]]
                        if idx >= 2:
                            dep.append(hkv_ev[idx - 2])
                        hm_last = S.step("pe", dep, mmkv)

                        def evkv(v, dt=dt, eh=eh, idx=idx):
                            slot = idx % 2
                            dst = Smaster[:, dt * D + eh * 512: dt * D + eh * 512 + 512]
                            return v.tensor_add(dst, dst, pkv[:, slot * 512:(slot + 1) * 512])
                        hkv_ev.append(S.step("dve", [hm_last, hspill], evkv))

                prev_kv_done = hkv_ev[-1]
                prev_kv_pe = hm_last
                prev_x_pe = hm_last
                prev_split = hxs
                prev_spk = hspk; prev_spv = hspv

            h_p1a_done = S.all_done()

        if upto == 'p1a':
            _finish(S, block); s_scope.close(); return nc

        def wr_cc(e, dma):
            for dt in range(ND):
                dma(out=cc_in[dt * 128:(dt + 1) * 128, :],
                    in_=Smaster[:, dt * D:(dt + 1) * D])
            for dt in range(ND):
                dma(out=cc_in[D:D + 1, dt * 128:(dt + 1) * 128],
                    in_=z_acc[:, dt:dt + 1])
        h_wrcc = S.step("gp", h_p1a_done, wr_cc, n_dma=16)

        _wr_val = h_wrcc[1]

        def _cc_emit(e):
            e.wait_ge(S.sem["gp"], _wr_val)
            e.collective_compute(
                "AllGather", ALU.bypass, replica_groups=REPLICA_GROUPS,
                ins=[cc_in[:]], outs=[cc_out[:]]).then_inc(S.sem["cc"], 1)
        block.gpsimd(_cc_emit)
        S.cnt["cc"] = 1
        h_cc = ("cc", 1)

        p1b = ExitStack()
        with p1b:
            wq_h = SB(p1b, "wq_h", [128, ND * D], BF16); wq_l = SB(p1b, "wq_l", [128, ND * D], BF16)
            ws2_scope = ExitStack()
            wstage2 = SB(ws2_scope, "wstage2", [128, ND * D])

            def ldwq(e, dma):
                dma(out=wstage2[:].rearrange("p (dt o) -> p dt o", dt=ND),
                    in_=WT_d["WqT"].rearrange("(dt p) o -> p dt o", p=128))
            hlq = S.step("sp", h_p1a_done, ldwq, n_dma=1)

            def splwq(v):
                v.tensor_copy(wq_h[:], wstage2[:])
                return v.tensor_sub(wq_l[:], wstage2[:], wq_h[:])
            hwq = S.step("dve", [hlq], splwq)
            ws2_scope.close()

            xT_st2 = SB(p1b, "xT_st2", [128, ND * GT])
            xh2 = SB(p1b, "xh2", [128, ND * GT], BF16); xl2 = SB(p1b, "xl2", [128, ND * GT], BF16)
            qD_f = SB(p1b, "qD_f", [128, ND * GT])
            q_h = SB(p1b, "q_h", [128, ND * GT], BF16); q_l = SB(p1b, "q_l", [128, ND * GT], BF16)
            ephi2 = SB(p1b, "ephi2", [128, GT]); rphi2 = SB(p1b, "rphi2", [128, GT])
            pq = PS(p1b, "pq", [128, 2 * GT])

            prev_split = hwq
            prev_x_pe = None
            prev_spq = None
            for g in range(NG):
                def ldx2(e, dma, g=g):
                    dma(out=xT_st2[:].rearrange("p (dt t) -> p dt t", dt=ND),
                        in_=xT_d[:, g * GT:(g + 1) * GT]
                        .rearrange("(dt p) t -> p dt t", p=128))
                hx = S.step("sp", [prev_split], ldx2, n_dma=1)

                def splx2(v):
                    v.tensor_copy(xh2[:], xT_st2[:])
                    return v.tensor_sub(xl2[:], xT_st2[:], xh2[:])
                hxs = S.step("dve", [hx, hwq, prev_x_pe], splx2)

                hq_evac = []
                hmq = None
                for ot in range(ND):
                    def mmq(p, ot=ot):
                        pairs = []
                        for dk in range(ND):
                            lh = wq_h[:, dk * D + ot * 128: dk * D + ot * 128 + 128]
                            ll = wq_l[:, dk * D + ot * 128: dk * D + ot * 128 + 128]
                            rh = xh2[:, dk * GT:(dk + 1) * GT]
                            rl = xl2[:, dk * GT:(dk + 1) * GT]
                            pairs += hl3(lh, ll, rh, rl)
                        return mm_acc(p, pq[:, (ot % 2) * GT:(ot % 2 + 1) * GT], pairs)
                    dep = [hxs] + ([hq_evac[ot - 2][1]] if ot >= 2 else [])
                    hmq = S.step("pe", dep, mmq)

                    def eq_act(a, ot=ot):
                        return a.activation(ephi2[:], pq[:, (ot % 2) * GT:(ot % 2 + 1) * GT],
                                            ACT.Exp, bias=bq_s[:, ot:ot + 1])
                    heq = S.step("act", [hmq] + ([hq_evac[ot - 1][1]] if ot >= 1 else []),
                                 eq_act)

                    def eq_dve(v, ot=ot):
                        v.tensor_scalar(rphi2[:], pq[:, (ot % 2) * GT:(ot % 2 + 1) * GT],
                                        bq_s[:, ot:ot + 1], 0.0, ALU.add, ALU.max)
                        v.scalar_tensor_tensor(qD_f[:, ot * GT:(ot + 1) * GT],
                                               ephi2[:], 1.0, rphi2[:], ALU.min, ALU.add)
                        v.tensor_copy(q_h[:, ot * GT:(ot + 1) * GT],
                                      qD_f[:, ot * GT:(ot + 1) * GT])
                        return v.tensor_sub(q_l[:, ot * GT:(ot + 1) * GT],
                                            qD_f[:, ot * GT:(ot + 1) * GT],
                                            q_h[:, ot * GT:(ot + 1) * GT])
                    hev = S.step("dve", [hmq, heq, prev_spq, prev_x_pe], eq_dve)
                    hq_evac.append((hmq, hev))

                def spq(e, dma, g=g):
                    dma(out=spill_q[g, 0], in_=q_h[:])
                    dma(out=spill_q[g, 1], in_=q_l[:])
                prev_spq = S.step("gp", [hq_evac[-1][1]], spq, n_dma=2)
                prev_split = hxs
                prev_x_pe = hmq

            h_p1b_done = S.all_done()

        if upto == 'p1b':
            _finish(S, block); s_scope.close(); return nc

        px = ExitStack()
        with px:
            Ttmp = SB(px, "Ttmp", [128, ND * D])
            ztmp8 = SB(px, "ztmp8", [128, ND])
            hacc = None
            for r in range(4):
                def ldT(e, dma, r=r):
                    base = r * (D + 1)
                    dma(out=Ttmp[:].rearrange("p (dt e) -> p dt e", dt=ND),
                        in_=cc_out[base:base + D, :].rearrange("(dt p) e -> p dt e", p=128))
                    dma(out=ztmp8[:],
                        in_=cc_out[base + D:base + D + 1, :]
                        .rearrange("r (dt p) -> p (r dt)", p=128, r=1),
                        allow_slow_non_contiguous=True)
                dep = [h_cc] + ([hacc] if hacc else []) + (h_p1b_done if r == 0 else [])
                hl = S.step("sp", dep, ldT, n_dma=2)

                def accT(v, r=r):
                    if r == 0:
                        v.tensor_scalar(Smaster[:], Ttmp[:], pmask_s[:, 0:1], None, ALU.mult)
                        return v.tensor_scalar(z_start[:], ztmp8[:], pmask_s[:, 0:1],
                                               None, ALU.mult)
                    v.scalar_tensor_tensor(Smaster[:], Ttmp[:], pmask_s[:, r:r + 1],
                                           Smaster[:], ALU.mult, ALU.add)
                    return v.scalar_tensor_tensor(z_start[:], ztmp8[:], pmask_s[:, r:r + 1],
                                                  z_start[:], ALU.mult, ALU.add)
                hacc = S.step("dve", [hl] + h_p1b_done, accT)

            def spss(e, dma):
                dma(out=sstart_d[:], in_=Smaster[:])
            h_spss = S.step("gp", [hacc], spss, n_dma=1)
            h_x2_done = S.all_done()
        s_scope.close()
        if upto == 'x2':
            _finish(S, block); return nc

        p2 = ExitStack()
        with p2:
            wo_h = SB(p2, "wo_h", [128, ND * D], BF16); wo_l = SB(p2, "wo_l", [128, ND * D], BF16)
            ws3_scope = ExitStack()
            wstage3 = SB(ws3_scope, "wstage3", [128, ND * D])

            def ldwo(e, dma):
                dma(out=wstage3[:].rearrange("p (dt o) -> p dt o", dt=ND),
                    in_=WT_d["WoT"].rearrange("(dt p) o -> p dt o", p=128))
            hlo = S.step("sp", h_x2_done, ldwo, n_dma=1)

            def splwo(v):
                v.tensor_copy(wo_h[:], wstage3[:])
                return v.tensor_sub(wo_l[:], wstage3[:], wo_h[:])
            hwo = S.step("dve", [hlo], splwo)
            ws3_scope.close()
            if p2sub == 'w':
                _finish(S, block)
                return nc

            Sgrp = SB(p2, "Sgrp", [128, ND * D])
            s_h = SB(p2, "s_h", [128, ND * D], BF16); s_l = SB(p2, "s_l", [128, ND * D], BF16)
            q2_h = SB(p2, "q2_h", [128, ND * GT], BF16); q2_l = SB(p2, "q2_l", [128, ND * GT], BF16)
            k2_h = SB(p2, "k2_h", [128, G * D], BF16); k2_l = SB(p2, "k2_l", [128, G * D], BF16)
            v2_h = SB(p2, "v2_h", [128, G * D], BF16); v2_l = SB(p2, "v2_l", [128, G * D], BF16)
            kd_h = SB(p2, "kd_h", [128, ND * 128], BF16); kd_l = SB(p2, "kd_l", [128, ND * 128], BF16)
            aT_h = SB(p2, "aT_h", [128, G * GT], BF16); aT_l = SB(p2, "aT_l", [128, G * GT], BF16)
            atmp = SB(p2, "atmp", [128, 128])
            dcol = SB(p2, "dcol", [128, G])
            recip_den = SB(p2, "recip_den", [128, G])
            o_h = SB(p2, "o_h", [128, D], BF16); o_l = SB(p2, "o_l", [128, D], BF16)
            oD_h = SB(p2, "oD_h", [128, ND * 128], BF16); oD_l = SB(p2, "oD_l", [128, ND * 128], BF16)
            x_res = SB(p2, "x_res", [128, D])
            y_t = SB(p2, "y_t", [128, D])
            y1_t = SB(p2, "y1_t", [128, D])
            zgtmp = SB(p2, "zgtmp", [128, ND])
            bnb = SB(p2, "bnb", [128, 12]); bna = SB(p2, "bna", [128, 2])
            sd_c = SB(p2, "sd_c", [128, 1]); rstd_c = SB(p2, "rstd_c", [128, 1])
            negm_c = SB(p2, "negm_c", [128, 1])
            pA = PS(p2, "pA", [128, 512])
            pden = PS(p2, "pden", [128, G])
            pnum = PS(p2, "pnum", [128, D])
            pwo = PS(p2, "pwo", [128, D])
            ptr2b = PS(p2, "ptr2b", [128, 128], BF16)

            _p2stop = False
            h_prev_group_end = None
            prev_hey = None
            prev_hsd = None
            prev_sty1 = None
            prev_hxr_reader = None
            prev_ot_ev = None
            prev_o_pe = None
            for g in range(NG if not p2sub else 1):
              try:
                  def ldg(e, dma, g=g):
                      dma(out=Sgrp[:], in_=prefix_s[g])
                      dma(out=zgtmp[:], in_=prefix_z[g])
                      dma(out=q2_h[:], in_=spill_q[g, 0]); dma(out=q2_l[:], in_=spill_q[g, 1])
                      dma(out=k2_h[:], in_=spill_k[g, 0]); dma(out=k2_l[:], in_=spill_k[g, 1])
                      dma(out=v2_h[:], in_=spill_v[g, 0]); dma(out=v2_l[:], in_=spill_v[g, 1])
                  dep = h_x2_done + [hwo] if g == 0 else [h_prev_group_end]
                  hld = S.step("sp", dep, ldg, n_dma=8)
                  if p2sub == 'ld0':
                      raise _StopEmit

                  def accS(e, dma):
                      for i in range(4):
                          dma(out=Sgrp[:, i * 2048:(i + 1) * 2048],
                              in_=sstart_d[:, i * 2048:(i + 1) * 2048],
                              accum_op=ALU.add)
                  haccS = S.step("gp", [hld], accS, n_dma=4)
                  if p2sub == 'ld1':
                      raise _StopEmit

                  def mks(v):
                      v.tensor_copy(s_h[:], Sgrp[:])
                      v.tensor_sub(s_l[:], Sgrp[:], s_h[:])
                      v.tensor_add(zg[:], zgtmp[:], z_start[:])
                      v.tensor_copy(zg_h[:], zg[:])
                      return v.tensor_sub(zg_l[:], zg[:], zg_h[:])
                  hS = S.step("dve", [hld, haccS], mks)
                  if p2sub == 'ld':
                      raise _StopEmit

                  hA_ev = []
                  for st_ in range(G):
                      htr_ev = None
                      for (ksrc, kdst) in ((k2_h, kd_h), (k2_l, kd_l)):
                          for dk in range(ND):
                              def trk2(p, st_=st_, dk=dk, ksrc=ksrc):
                                  return p.transpose(
                                      ptr2b[:],
                                      ksrc[:, st_ * D + dk * 128: st_ * D + dk * 128 + 128],
                                      ident_b[:])
                              dep = [hld]
                              if htr_ev is not None:
                                  dep.append(htr_ev)
                              elif hA_ev:
                                  dep.append(hA_ev[-1])
                              elif prev_ot_ev is not None:
                                  dep.append(prev_ot_ev)
                              ht = S.step("pe", dep, trk2)

                              def evt(v, dk=dk, kdst=kdst):
                                  return v.tensor_copy(kdst[:, dk * 128:(dk + 1) * 128],
                                                       ptr2b[:])
                              htr_ev = S.step("dve", [ht], evt)

                      ncols = GT - st_ * 128

                      def mma(p, st_=st_, ncols=ncols):
                          pairs = []
                          for dk in range(ND):
                              lh = kd_h[:, dk * 128:(dk + 1) * 128]
                              ll = kd_l[:, dk * 128:(dk + 1) * 128]
                              rh = q2_h[:, dk * GT + st_ * 128: dk * GT + GT]
                              rl = q2_l[:, dk * GT + st_ * 128: dk * GT + GT]
                              pairs += hl3(lh, ll, rh, rl)
                          return mm_acc(p, pA[:, 0:ncols], pairs)
                      hma = S.step("pe", [htr_ev, hS], mma)

                      def eva(v, st_=st_, ncols=ncols):
                          base = st_ * GT + st_ * 128
                          v.tensor_mul(atmp[:], pA[:, 0:128], triu_f[:])
                          v.tensor_copy(aT_h[:, base:base + 128], atmp[:])
                          v.scalar_tensor_tensor(aT_l[:, base:base + 128], atmp[:],
                                                 1.0, aT_h[:, base:base + 128],
                                                 ALU.mult, ALU.subtract)
                          if ncols > 128:
                              v.tensor_copy(aT_h[:, base + 128: st_ * GT + GT],
                                            pA[:, 128:ncols])
                              return v.scalar_tensor_tensor(
                                  aT_l[:, base + 128: st_ * GT + GT], pA[:, 128:ncols],
                                  1.0, aT_h[:, base + 128: st_ * GT + GT],
                                  ALU.mult, ALU.subtract)
                          return v.engine_nop()
                      hA_ev.append(S.step("dve", [hma], eva))
                  if p2sub == 'A':
                      raise _StopEmit

                  def mden(p):
                      last = None
                      first = True
                      for cq in range(G):
                          out = pden[:, cq:cq + 1]
                          for st_ in range(cq + 1):
                              base = st_ * GT + cq * 128
                              for pl in (aT_h, aT_l):
                                  last = p.matmul(out, pl[:, base:base + 128],
                                                  ones_bf[:, 0:1],
                                                  start=first, stop=False)
                                  first = False
                          for dk in range(ND):
                              qh_sl = q2_h[:, dk * GT + cq * 128: dk * GT + cq * 128 + 128]
                              ql_sl = q2_l[:, dk * GT + cq * 128: dk * GT + cq * 128 + 128]
                              for (qp, zp) in ((qh_sl, zg_h), (ql_sl, zg_h), (qh_sl, zg_l)):
                                  isl = (cq == G - 1 and dk == ND - 1 and
                                         qp is qh_sl and zp is zg_l)
                                  last = p.matmul(out, qp, zp[:, dk:dk + 1],
                                                  start=False, stop=isl)
                      return last
                  hmden = S.step("pe", hA_ev + [hS], mden)

                  def evden(v):
                      v.tensor_scalar(dcol[:], pden[:], 1e-6, None, ALU.add)
                      return v.reciprocal(recip_den[:], dcol[:])
                  hevden = S.step("dve", [hmden], evden)
                  hrec = [(hevden, hevden)] * G
                  if p2sub == 'den':
                      raise _StopEmit

                  for cq in range(G):
                      def mnum(p, cq=cq):
                          lastl = None
                          for eh in range(2):
                              pairs = []
                              for dk in range(ND):
                                  qh_sl = q2_h[:, dk * GT + cq * 128: dk * GT + cq * 128 + 128]
                                  ql_sl = q2_l[:, dk * GT + cq * 128: dk * GT + cq * 128 + 128]
                                  sh_sl = s_h[:, dk * D + eh * 512: dk * D + eh * 512 + 512]
                                  sl_sl = s_l[:, dk * D + eh * 512: dk * D + eh * 512 + 512]
                                  pairs += hl3(qh_sl, ql_sl, sh_sl, sl_sl)
                              for st_ in range(cq + 1):
                                  ah_sl = aT_h[:, st_ * GT + cq * 128: st_ * GT + cq * 128 + 128]
                                  al_sl = aT_l[:, st_ * GT + cq * 128: st_ * GT + cq * 128 + 128]
                                  vh_sl = v2_h[:, st_ * D + eh * 512: st_ * D + eh * 512 + 512]
                                  vl_sl = v2_l[:, st_ * D + eh * 512: st_ * D + eh * 512 + 512]
                                  pairs += hl3(ah_sl, al_sl, vh_sl, vl_sl)
                              lastl = mm_acc(p, pnum[:, eh * 512:(eh + 1) * 512], pairs)
                          return lastl
                      dep = [hS, hA_ev[-1], hrec[-1][1]]
                      if cq > 0:
                          dep.append(h_prev_evo)
                      hmn = S.step("pe", dep, mnum)

                      def evo(v, cq=cq):
                          v.tensor_scalar(o_h[:], pnum[:], recip_den[:, cq:cq + 1],
                                          None, ALU.mult)
                          return v.scalar_tensor_tensor(
                              o_l[:], pnum[:], recip_den[:, cq:cq + 1], o_h[:],
                              ALU.mult, ALU.subtract)
                      hoe = S.step("dve", [hmn, hrec[cq][1], prev_o_pe], evo)
                      h_prev_evo = hoe
                      if p2sub == 'num':
                          raise _StopEmit

                      hot_ev = None
                      for (osrc, odst) in ((o_h, oD_h), (o_l, oD_l)):
                          for dt in range(ND):
                              def tro(p, dt=dt, osrc=osrc):
                                  return p.transpose(ptr2b[:],
                                                     osrc[:, dt * 128:(dt + 1) * 128],
                                                     ident_b[:])
                              dep = [hoe]
                              if hot_ev is not None:
                                  dep.append(hot_ev)
                              elif prev_ot_ev is not None:
                                  dep.append(prev_ot_ev)
                              hto = S.step("pe", dep, tro)

                              def evot(v, dt=dt, odst=odst):
                                  return v.tensor_copy(odst[:, dt * 128:(dt + 1) * 128],
                                                       ptr2b[:])
                              hot_ev = S.step("dve", [hto], evot)
                      prev_ot_ev = hot_ev

                      def mwo(p, cq=cq):
                          last = None
                          for eh in range(2):
                              pairs = []
                              for ek in range(ND):
                                  lh = oD_h[:, ek * 128:(ek + 1) * 128]
                                  ll = oD_l[:, ek * 128:(ek + 1) * 128]
                                  rh = wo_h[:, ek * D + eh * 512: ek * D + eh * 512 + 512]
                                  rl = wo_l[:, ek * D + eh * 512: ek * D + eh * 512 + 512]
                                  pairs += hl3(lh, ll, rh, rl)
                              last = mm_acc(p, pwo[:, eh * 512:(eh + 1) * 512], pairs)
                          return last
                      hmwo = S.step("pe", [hot_ev, prev_hey], mwo)
                      prev_o_pe = hmwo
                      if p2sub == 'wo':
                          raise _StopEmit

                      def ldxr(e, dma, g=g, cq=cq):
                          t0 = (g * G + cq) * 128
                          dma(out=x_res[:], in_=x_d[t0:t0 + 128, :])
                      hxr = S.step("sp", [prev_hxr_reader], ldxr, n_dma=1)

                      def evy(v):
                          v.tensor_add(y_t[:], pwo[:], bob_s[:])
                          v.tensor_add(y_t[:], y_t[:], x_res[:])
                          v.bn_stats(bnb[:, 0:6], y_t[:, 0:512])
                          v.bn_stats(bnb[:, 6:12], y_t[:, 512:1024])
                          v.bn_aggr(bna[:, 0:2], bnb[:, 0:12])
                          return v.tensor_scalar(negm_c[:], bna[:, 0:1], -1.0, None, ALU.mult)
                      hey = S.step("dve", [hmwo, hxr, prev_hsd], evy)
                      prev_hey = hey
                      prev_hxr_reader = hey

                      hsd = S.step("act", [hey], lambda a: a.activation(
                          sd_c[:], bna[:, 1:2], ACT.Sqrt, bias=eps_c[:, 0:1]))
                      prev_hsd = hsd

                      def evy2(v):
                          v.reciprocal(rstd_c[:], sd_c[:])
                          return v.tensor_scalar(y1_t[:], y_t[:], negm_c[:, 0:1],
                                                 rstd_c[:, 0:1], ALU.add, ALU.mult)
                      hy1 = S.step("dve", [hsd, prev_sty1], evy2)

                      def sty1(e, dma, g=g, cq=cq):
                          t0 = (g * G + cq) * 128
                          dma(out=y1_dram[t0:t0 + 128, :], in_=y1_t[:])
                      prev_sty1 = S.step("gp", [hy1], sty1, n_dma=1)

                  h_prev_group_end = hy1
              except _StopEmit:
                _p2stop = True

            h_p2_done = S.all_done()
            if p2sub:
                pass

        if upto == 'p2':
            _finish(S, block); return nc

        p3a = ExitStack()
        with p3a:
            w1q_s = SB(p3a, "w1q_s", [128, ND * H], BF16)
            y1_ld = SB(p3a, "y1_ld", [128, D])
            m1c = SB(p3a, "m1c", [128, 1]); r1c = SB(p3a, "r1c", [128, 1])
            f1c = SB(p3a, "f1c", [128, 1])
            m2c = SB(p3a, "m2c", [128, 1]); r2c = SB(p3a, "r2c", [128, 1])
            mtmp = SB(p3a, "mtmp", [128, 1])
            xi_ = SB(p3a, "xi_", [128, D], I32)
            xb_ = SB(p3a, "xb_", [128, D], BF16)
            yqD = SB(p3a, "yqD", [128, ND * 128], BF16)
            h_f = SB(p3a, "h_f", [128, H])
            hi_ = SB(p3a, "hi_", [128, H], I32)
            hb_ = SB(p3a, "hb_", [128, H], BF16)
            ph_ = PS(p3a, "ph_", [128, 2 * 512])
            ptr3 = PS(p3a, "ptr3", [128, 128], BF16)

            hw1 = S.step("sp", h_p2_done, lambda e, dma: dma(out=w1q_s[:], in_=w1q_dram[:])
                         and None, n_dma=1)

            prev_hq2 = None
            prev_trq_pe = None
            prev_gel_last = None
            prev_sth = None
            prev_hly_gate = hw1
            for ti in range(NT):
                def ldy(e, dma, ti=ti):
                    dma(out=y1_ld[:], in_=y1_dram[ti * 128:(ti + 1) * 128, :])
                hly = S.step("sp", [prev_hly_gate], ldy, n_dma=1)

                def q1(v):
                    v.reduce_max(m1c[:], y1_ld[:], axis=AX.X, apply_absolute_value=True)
                    v.tensor_scalar(m1c[:], m1c[:], 1e-5, None, ALU.max)
                    v.reciprocal(r1c[:], m1c[:])
                    v.tensor_scalar(f1c[:], m1c[:], s1bc[:, 0:1], 1.0 / 127.0,
                                    ALU.mult, ALU.mult)
                    v.tensor_scalar(xi_[:], y1_ld[:], r1c[:, 0:1], 127.0,
                                    ALU.mult, ALU.mult)
                    return v.tensor_copy(xb_[:], xi_[:])
                hq1 = S.step("dve", [hly, prev_trq_pe, prev_gel_last], q1)

                htr_ev = None
                htr_pe = None
                for dt in range(ND):
                    def trq(p, dt=dt):
                        return p.transpose(ptr3[:], xb_[:, dt * 128:(dt + 1) * 128],
                                           ident_b[:])
                    dep = [hq1] + ([htr_ev] if htr_ev else [])
                    htr_pe = S.step("pe", dep, trq)

                    def evq(v, dt=dt):
                        return v.tensor_copy(yqD[:, dt * 128:(dt + 1) * 128], ptr3[:])
                    htr_ev = S.step("dve", [htr_pe], evq)
                prev_trq_pe = htr_pe

                hml = []
                for hh in range(8):
                    def mm1(p, hh=hh):
                        pairs = [(yqD[:, dk * 128:(dk + 1) * 128],
                                  w1q_s[:, dk * H + hh * 512: dk * H + hh * 512 + 512])
                                 for dk in range(ND)]
                        return mm_acc(p, ph_[:, (hh % 2) * 512:(hh % 2 + 1) * 512], pairs)
                    dep = [htr_ev, hw1] + ([hml[-2][1]] if len(hml) >= 2 else [])
                    hm = S.step("pe", dep, mm1)

                    def gel(a, hh=hh):
                        return a.activation(h_f[:, hh * 512:(hh + 1) * 512],
                                            ph_[:, (hh % 2) * 512:(hh % 2 + 1) * 512],
                                            ACT.Gelu, scale=f1c[:, 0:1])
                    gdep = [hm] + ([prev_hq2] if (hh == 0 and prev_hq2) else [])
                    hg = S.step("act", gdep, gel)

                    def mx(v, hh=hh):
                        if hh == 0:
                            return v.reduce_max(m2c[:], h_f[:, 0:512], axis=AX.X,
                                                apply_absolute_value=True)
                        v.reduce_max(mtmp[:], h_f[:, hh * 512:(hh + 1) * 512], axis=AX.X,
                                     apply_absolute_value=True)
                        return v.tensor_max(m2c[:], m2c[:], mtmp[:])
                    hx_ = S.step("dve", [hg], mx)
                    hml.append((hm, hg, hx_))
                prev_gel_last = hml[-1][1]

                def q2(v, ti=ti):
                    v.tensor_scalar(m2c[:], m2c[:], 1e-5, None, ALU.max)
                    v.reciprocal(r2c[:], m2c[:])
                    v.tensor_scalar(f2all[:, ti:ti + 1], m2c[:], s2bc[:, 0:1], 1.0 / 127.0,
                                    ALU.mult, ALU.mult)
                    v.tensor_scalar(hi_[:], h_f[:], r2c[:, 0:1], 127.0, ALU.mult, ALU.mult)
                    return v.tensor_copy(hb_[:], hi_[:])
                hq2 = S.step("dve", [hml[-1][2], prev_sth], q2)
                prev_hq2 = hq2

                def sth(e, dma, ti=ti):
                    dma(out=hq_dram[ti], in_=hb_[:])
                prev_sth = S.step("gp", [hq2], sth, n_dma=1)
                prev_hly_gate = hq1

            h_p3a_done = S.all_done()

        if upto == 'p3a':
            _finish(S, block); return nc

        p3b = ExitStack()
        with p3b:
            w2q_s = SB(p3b, "w2q_s", [128, NH * D], BF16)
            hb2 = SB(p3b, "hb2", [128, H], BF16)
            y1_ld2 = SB(p3b, "y1_ld2", [128, D])
            hqD = SB(p3b, "hqD", [128, NH * 128], BF16)
            y2_t = SB(p3b, "y2_t", [128, D])
            out_t = SB(p3b, "out_t", [128, D])
            bnb2 = SB(p3b, "bnb2", [128, 12]); bna2 = SB(p3b, "bna2", [128, 2])
            sd2 = SB(p3b, "sd2", [128, 1]); rstd2 = SB(p3b, "rstd2", [128, 1])
            negm2 = SB(p3b, "negm2", [128, 1])
            po_ = PS(p3b, "po_", [128, D])
            ptr4 = PS(p3b, "ptr4", [128, 128], BF16)

            hw2 = S.step("sp", h_p3a_done, lambda e, dma: dma(out=w2q_s[:], in_=w2q_dram[:])
                         and None, n_dma=1)

            prev_hf = None
            prev_hm2 = None
            prev_hs2 = None
            prev_stout = None
            h_last_out = None
            for ti in range(NT):
                def ldh(e, dma, ti=ti):
                    dma(out=hb2[:], in_=hq_dram[ti])
                    dma(out=y1_ld2[:], in_=y1_dram[ti * 128:(ti + 1) * 128, :])
                dep = [hw2] if ti == 0 else [prev_hf, prev_hm2]
                hlh = S.step("sp", dep, ldh, n_dma=2)

                htr_ev = None
                for hk in range(NH):
                    def trh(p, hk=hk):
                        return p.transpose(ptr4[:], hb2[:, hk * 128:(hk + 1) * 128],
                                           ident_b[:])
                    dep = [hlh] + ([htr_ev] if htr_ev else [])
                    ht = S.step("pe", dep, trh)

                    def evh(v, hk=hk):
                        return v.tensor_copy(hqD[:, hk * 128:(hk + 1) * 128], ptr4[:])
                    htr_ev = S.step("dve", [ht], evh)

                def mm2(p):
                    last = None
                    for eh in range(2):
                        pairs = [(hqD[:, hk * 128:(hk + 1) * 128],
                                  w2q_s[:, hk * D + eh * 512: hk * D + eh * 512 + 512])
                                 for hk in range(NH)]
                        last = mm_acc(p, po_[:, eh * 512:(eh + 1) * 512], pairs)
                    return last
                hm2 = S.step("pe", [htr_ev, hw2], mm2)
                prev_hm2 = hm2

                def evy2b(v, ti=ti):
                    v.scalar_tensor_tensor(y2_t[:], po_[:], f2all[:, ti:ti + 1],
                                           y1_ld2[:], ALU.mult, ALU.add)
                    v.bn_stats(bnb2[:, 0:6], y2_t[:, 0:512])
                    v.bn_stats(bnb2[:, 6:12], y2_t[:, 512:1024])
                    v.bn_aggr(bna2[:, 0:2], bnb2[:, 0:12])
                    return v.tensor_scalar(negm2[:], bna2[:, 0:1], -1.0, None, ALU.mult)
                he2 = S.step("dve", [hm2, hlh, prev_hs2], evy2b)

                hs2 = S.step("act", [he2], lambda a: a.activation(
                    sd2[:], bna2[:, 1:2], ACT.Sqrt, bias=eps_c[:, 0:1]))
                prev_hs2 = hs2

                def fin(v):
                    v.reciprocal(rstd2[:], sd2[:])
                    return v.tensor_scalar(out_t[:], y2_t[:], negm2[:, 0:1],
                                           rstd2[:, 0:1], ALU.add, ALU.mult)
                hf = S.step("dve", [hs2, prev_stout], fin)
                prev_hf = hf

                def stout(e, dma, ti=ti):
                    dma(out=out_d[ti * 128:(ti + 1) * 128, :], in_=out_t[:])
                h_last_out = S.step("gp", [hf], stout, n_dma=1)
                prev_stout = h_last_out

            tgt = S.cnt["gp"]

            def gp_final(e):
                e.wait_ge(sem["gp"], tgt)
            block.gpsimd(gp_final)

    return nc


_CACHE = {}


def _get_nc(dbg=False):
    key = bool(dbg)
    if key not in _CACHE:
        _CACHE[key] = build_program(dbg=dbg)
    return _CACHE[key]


def _prep_inputs(inputs):
    x = np.asarray(inputs["L0_attn_x"], np.float32)
    Wq = np.asarray(inputs["L0_attn_Wq"], np.float32)
    bq = np.asarray(inputs["L0_attn_bq"], np.float32)
    Wk = np.asarray(inputs["L0_attn_Wk"], np.float32)
    bk = np.asarray(inputs["L0_attn_bk"], np.float32)
    Wv = np.asarray(inputs["L0_attn_Wv"], np.float32)
    bv = np.asarray(inputs["L0_attn_bv"], np.float32)
    Wo = np.asarray(inputs["L0_attn_Wo"], np.float32)
    bo = np.asarray(inputs["L0_attn_bo"], np.float32)
    w1 = np.asarray(inputs["L0_ffn_w1"], np.float32)
    w2 = np.asarray(inputs["L0_ffn_w2"], np.float32)

    shared = {
        "WqT": np.ascontiguousarray(Wq.T), "WkT": np.ascontiguousarray(Wk.T),
        "WvT": np.ascontiguousarray(Wv.T), "WoT": np.ascontiguousarray(Wo.T),
        "bq_t": np.ascontiguousarray(bq.reshape(ND, 128).T),
        "bk_t": np.ascontiguousarray(bk.reshape(ND, 128).T),
        "bv_b": np.ascontiguousarray(np.broadcast_to(bv, (128, D))),
        "bo_b": np.ascontiguousarray(np.broadcast_to(bo, (128, D))),
        "w1T": np.ascontiguousarray(w1.T), "w2T": np.ascontiguousarray(w2.T),
        "triu": np.triu(np.ones((128, 128), np.float32)),
        "ident": np.eye(128, dtype=np.float32),
    }
    in_maps = []
    for core in range(8):
        b, q = divmod(core, 4)
        seg = x[b, q * TSEG:(q + 1) * TSEG, :]
        m = dict(shared)
        m["x_seg"] = np.ascontiguousarray(seg)
        m["xT_seg"] = np.ascontiguousarray(seg.T)
        pm = (np.arange(4) < q).astype(np.float32)
        m["pmask"] = np.ascontiguousarray(np.broadcast_to(pm, (128, 4)))
        in_maps.append(m)
    return in_maps


def kernel(**inputs):
    in_maps = _prep_inputs(inputs)
    nc = _get_nc(dbg=False)
    res = run_bass_kernel_spmd(nc, in_maps, core_ids=list(range(8)))
    out = np.empty((B, T, D), np.float32)
    for core in range(8):
        b, q = divmod(core, 4)
        out[b, q * TSEG:(q + 1) * TSEG, :] = np.asarray(res.results[core]["out"])
    return out


# version 03
# speedup vs baseline: 1.0910x; 1.0910x over previous
import numpy as np
import ml_dtypes
from contextlib import ExitStack
import concourse.bass as bass
import concourse.mybir as mybir
from concourse.bass_utils import run_bass_kernel_spmd

F32 = mybir.dt.float32
BF16 = mybir.dt.bfloat16
I32 = mybir.dt.int32
AX = mybir.AxisListType
ALU = mybir.AluOpType
ACT = mybir.ActivationFunctionType

B, T, D = 2, 16384, 1024
H = 4 * D
TSEG = T // 4
C = 128
G = 4
GT = G * C
NG = TSEG // GT
ND = D // 128
NH = H // 128
NT = TSEG // C
REPLICA_GROUPS = [[0, 1, 2, 3], [4, 5, 6, 7]]

ENG_ATTR = {"pe": "tensor", "dve": "vector", "act": "scalar", "sp": "sync", "gp": "gpsimd"}


class _SerialEng:

    _PASSTHRU = {"wait_ge"}

    def __init__(self, eng, sem, state):
        self._eng = eng
        self._sem = sem
        self._state = state

    def __getattr__(self, name):
        fn = getattr(self._eng, name)
        if name in self._PASSTHRU or not callable(fn):
            return fn

        def wrapped(*a, **kw):
            st = self._state
            if st["issued"] > st["waited"]:
                self._eng.wait_ge(self._sem, st["issued"])
                st["waited"] = st["issued"]
            inst = fn(*a, **kw)
            inst.then_inc(self._sem, 1)
            st["issued"] += 1
            return inst
        return wrapped


class Sched:
    ENGS = ("pe", "dve", "act", "sp", "gp")
    SERIAL = {"dve", "act"}

    def __init__(self, nc, block, es):
        self.nc = nc
        self.block = block
        self.sem = {e: es.enter_context(nc.semaphore("sem_" + e)) for e in self.ENGS}
        self.sem["cc"] = es.enter_context(nc.semaphore("sem_cc"))
        self.cnt = {e: 0 for e in self.ENGS}
        self.cnt["cc"] = 0
        self.waited = {e: {e2: 0 for e2 in (*self.ENGS, "cc")} for e in self.ENGS}
        self.serial_state = {e: {"issued": 0, "waited": 0} for e in self.SERIAL}

    def step(self, eng, deps, body, n_dma=0):
        sem = self.sem
        waited = self.waited[eng]
        serial = eng in self.SERIAL
        use = []
        for d in deps:
            if d is None:
                continue
            de, dv = d
            if de == eng:
                if serial:
                    continue
            if dv > waited[de]:
                if de == eng and not n_dma and not serial:
                    continue
                use.append((de, dv))
                waited[de] = dv
        n_body_inst = [0]
        own_wait = 0
        if n_dma:
            own_wait = self.cnt[eng]
            if own_wait <= waited[eng]:
                own_wait = 0
            else:
                waited[eng] = own_wait
            self.cnt[eng] += 16 * n_dma
        issued = [0]

        def emit(e):
            if own_wait:
                e.wait_ge(sem[eng], own_wait)
            for de, dv in use:
                e.wait_ge(sem[de], dv)
            if n_dma:
                def dma(out, in_, **kw):
                    issued[0] += 1
                    return e.dma_start(out=out, in_=in_, **kw).then_inc(sem[eng], 16)
                body(e, dma)
                assert issued[0] == n_dma, f"declared {n_dma} dmas, issued {issued[0]}"
            elif serial:
                body(_SerialEng(e, sem[eng], self.serial_state[eng]))
            else:
                last = body(e)
                assert last is not None, "compute step must return its last instruction"
                last.then_inc(sem[eng], 1)

        if serial and not n_dma:
            getattr(self.block, ENG_ATTR[eng])(emit)
            self.cnt[eng] = self.serial_state[eng]["issued"]
            return (eng, self.cnt[eng])
        if not n_dma:
            self.cnt[eng] += 1
        getattr(self.block, ENG_ATTR[eng])(emit)
        return (eng, self.cnt[eng])

    def all_done(self):
        return [(e, self.cnt[e]) for e in self.ENGS]


def _finish(S, block):
    sem = S.sem
    cnts = {e: S.cnt[e] for e in S.ENGS}

    def gp_final(e):
        for e2 in S.ENGS:
            if cnts[e2]:
                e.wait_ge(sem[e2], cnts[e2])
    block.gpsimd(gp_final)


def mm_acc(pe, out, pairs):
    n = len(pairs)
    last = None
    for i, (l, r) in enumerate(pairs):
        last = pe.matmul(out, l, r, start=(i == 0), stop=(i == n - 1))
    return last


def hl3(lh, ll, rh, rl):
    return [(lh, rh), (lh, rl), (ll, rh)]


class _StopEmit(Exception):
    pass


def build_program(dbg=False, upto='p3b'):
    p2sub = upto.split('@', 1)[1] if upto.startswith('p2@') else None
    if p2sub:
        upto = 'p2'
    nc = bass.Bass("TRN2", target_bir_lowering=False)

    def dram_in(name, shape, dt=F32):
        return nc.dram_tensor(name, shape, dt, kind="ExternalInput").ap()

    def dram_scratch(name, shape, dt=F32):
        if dbg:
            return nc.dram_tensor(name, shape, dt, kind="ExternalOutput").ap()
        return nc.dram_tensor(name, shape, dt).ap()

    x_d = dram_in("x_seg", [TSEG, D])
    xT_d = dram_in("xT_seg", [D, TSEG])
    WT_d = {w: dram_in(w, [D, D]) for w in ("WqT", "WkT", "WvT", "WoT")}
    bq_d = dram_in("bq_t", [128, ND]); bk_d = dram_in("bk_t", [128, ND])
    bvb_d = dram_in("bv_b", [128, D]); bob_d = dram_in("bo_b", [128, D])
    w1T_d = dram_in("w1T", [D, H]); w2T_d = dram_in("w2T", [H, D])
    triu_d = dram_in("triu", [128, 128])
    ident_d = dram_in("ident", [128, 128])
    pmask_d = dram_in("pmask", [128, 4])
    out_d = nc.dram_tensor("out", [TSEG, D], F32, kind="ExternalOutput").ap()

    spill_k = dram_scratch("spill_k", [NG, 2, 128, G * D], BF16)
    spill_v = dram_scratch("spill_v", [NG, 2, 128, G * D], BF16)
    spill_q = dram_scratch("spill_q", [NG, 2, 128, ND * GT], BF16)
    prefix_s = dram_scratch("prefix_s", [NG, 128, ND * D])
    prefix_z = dram_scratch("prefix_z", [NG, 128, ND])
    y1_dram = dram_scratch("y1_spill", [TSEG, D])
    hq_dram = dram_scratch("hq_sp", [NT, 128, H], BF16)
    w1q_dram = dram_scratch("w1q", [128, ND * H], BF16)
    w2q_dram = dram_scratch("w2q", [128, NH * D], BF16)
    sstart_d = dram_scratch("sstart", [128, ND * D])
    cc_in = nc.dram_tensor("cc_in", [D + 1, D], F32).ap()
    cc_out = nc.dram_tensor("cc_out", [4 * (D + 1), D], F32).ap()

    es = ExitStack()
    with es:
        block = es.enter_context(nc.Block())
        S = Sched(nc, block, es)
        sem = S.sem

        def SB(scope, name, shape, dt=F32):
            return scope.enter_context(nc.sbuf_tensor(name, shape, dt))

        def PS(scope, name, shape, dt=F32):
            return scope.enter_context(nc.psum_tensor(name, shape, dt))

        ones_bf = SB(es, "ones_bf", [128, 128], BF16)
        ident_f = SB(es, "ident_f", [128, 128], F32)
        ident_b = SB(es, "ident_b", [128, 128], BF16)
        triu_f = SB(es, "triu_f", [128, 128], F32)
        s1bc = SB(es, "s1bc", [128, 1]); s2bc = SB(es, "s2bc", [128, 1])
        thr1 = SB(es, "thr1", [128, 1]); nthr1 = SB(es, "nthr1", [128, 1])
        thr2 = SB(es, "thr2", [128, 1]); nthr2 = SB(es, "nthr2", [128, 1])
        bq_s = SB(es, "bq_s", [128, ND]); bk_s = SB(es, "bk_s", [128, ND])
        bvb_s = SB(es, "bvb_s", [128, D]); bob_s = SB(es, "bob_s", [128, D])
        pmask_s = SB(es, "pmask_s", [128, 4])
        z_acc = SB(es, "z_acc", [128, ND]); z_start = SB(es, "z_start", [128, ND])
        zg = SB(es, "zg", [128, ND])
        zg_h = SB(es, "zg_h", [128, ND], BF16); zg_l = SB(es, "zg_l", [128, ND], BF16)
        f2all = SB(es, "f2all", [128, NT])
        eps_c = SB(es, "eps_c", [128, 1])
        s_scope = ExitStack()
        Smaster = SB(s_scope, "Smaster", [128, ND * D])

        def ld_const(e, dma):
            dma(out=ident_f[:], in_=ident_d[:])
            dma(out=triu_f[:], in_=triu_d[:])
            dma(out=bq_s[:], in_=bq_d[:])
            dma(out=bk_s[:], in_=bk_d[:])
            dma(out=bvb_s[:], in_=bvb_d[:])
            dma(out=bob_s[:], in_=bob_d[:])
            dma(out=pmask_s[:], in_=pmask_d[:])
        h_const = S.step("sp", [], ld_const, n_dma=7)

        def init_body(v):
            v.memset(ones_bf[:], 1.0)
            v.memset(eps_c[:], 1e-5)
            v.memset(z_acc[:], 0.0)
            v.memset(Smaster[:], 0.0)
            return v.tensor_copy(ident_b[:], ident_f[:])
        h_init = S.step("dve", [h_const], init_body)

        p0 = ExitStack()
        with p0:
            wpiece = SB(p0, "wpiece", [128, H])
            wsum = SB(p0, "wsum", [128, 1]); wtmp1 = SB(p0, "wtmp1", [128, 1])
            sh_ = SB(p0, "sh_", [128, 1], BF16); sl_ = SB(p0, "sl_", [128, 1], BF16)
            sc_row = SB(p0, "sc_row", [1, 2])
            sc_h = SB(p0, "sc_h", [1, 2], BF16); sc_l = SB(p0, "sc_l", [1, 2], BF16)
            cge = SB(p0, "cge", [128, H])
            tq_ = SB(p0, "tq_", [128, H], BF16)
            ps_s = PS(p0, "ps_s", [1, 2])
            ps_b = PS(p0, "ps_b", [128, 2])

            prev_wi_spill = None
            for wi, (wT_dd, npieces, pw, wq_dram, sbc, thr, nthr) in enumerate([
                    (w1T_d, ND, H, w1q_dram, s1bc, thr1, nthr1),
                    (w2T_d, NH, D, w2q_dram, s2bc, thr2, nthr2)]):
                hz = S.step("dve", [h_init], lambda v: v.memset(wsum[:], 0.0))
                hr = hz
                for pi in range(npieces):
                    def ldp(e, dma, pi=pi, wT_dd=wT_dd, pw=pw):
                        dma(out=wpiece[:, :pw], in_=wT_dd[pi * 128:(pi + 1) * 128, :])
                    hl = S.step("sp", [hr], ldp, n_dma=1)

                    def red(v, pw=pw):
                        v.reduce_sum(wtmp1[:], wpiece[:, :pw], axis=AX.X,
                                     apply_absolute_value=True)
                        return v.tensor_add(wsum[:], wsum[:], wtmp1[:])
                    hr = S.step("dve", [hl], red)

                def spl(v):
                    v.tensor_copy(sh_[:], wsum[:])
                    return v.tensor_sub(sl_[:], wsum[:], sh_[:])
                hsp = S.step("dve", [hr], spl)
                hmm = S.step("pe", [hsp], lambda p, wi=wi: mm_acc(
                    p, ps_s[0:1, wi:wi + 1],
                    [(sh_[:], ones_bf[:, 0:1]), (sl_[:], ones_bf[:, 0:1])]))
                nelem = float(npieces * 128 * pw)

                def mean(v, wi=wi, nelem=nelem):
                    v.tensor_scalar(sc_row[0:1, wi:wi + 1], ps_s[0:1, wi:wi + 1],
                                    1.0 / nelem, 1e-5, ALU.mult, ALU.max)
                    v.tensor_copy(sc_h[0:1, wi:wi + 1], sc_row[0:1, wi:wi + 1])
                    return v.tensor_sub(sc_l[0:1, wi:wi + 1], sc_row[0:1, wi:wi + 1],
                                        sc_h[0:1, wi:wi + 1])
                hmean = S.step("dve", [hmm], mean)
                hbc = S.step("pe", [hmean], lambda p, wi=wi: mm_acc(
                    p, ps_b[:, wi:wi + 1],
                    [(ones_bf[0:1, :], sc_h[0:1, wi:wi + 1]),
                     (ones_bf[0:1, :], sc_l[0:1, wi:wi + 1])]))

                def mkthr(v, sbc=sbc, thr=thr, nthr=nthr, wi=wi):
                    v.tensor_copy(sbc[:], ps_b[:, wi:wi + 1])
                    v.tensor_scalar(thr[:], sbc[:], 0.5, None, ALU.mult)
                    return v.tensor_scalar(nthr[:], sbc[:], -0.5, None, ALU.mult)
                hthr = S.step("dve", [hbc], mkthr)

                hq = hthr
                for pi in range(npieces):
                    def ldp2(e, dma, pi=pi, wT_dd=wT_dd, pw=pw):
                        dma(out=wpiece[:, :pw], in_=wT_dd[pi * 128:(pi + 1) * 128, :])
                    hl = S.step("sp", [hq, prev_wi_spill], ldp2, n_dma=1)

                    def qz(v, pw=pw, thr=thr, nthr=nthr):
                        v.tensor_scalar(cge[:, :pw], wpiece[:, :pw], nthr[:, 0:1], None,
                                        ALU.is_le)
                        return v.scalar_tensor_tensor(
                            tq_[:, :pw], wpiece[:, :pw], thr[:, 0:1], cge[:, :pw],
                            ALU.is_ge, ALU.subtract)
                    hcmp = S.step("dve", [hl], qz)

                    def stq(e, dma, pi=pi, wq_dram=wq_dram, pw=pw):
                        dma(out=wq_dram[:, pi * pw:(pi + 1) * pw], in_=tq_[:, :pw])
                    hq = S.step("gp", [hcmp], stq, n_dma=1)
                prev_wi_spill = hq
            h_p0_done = S.all_done()
        if upto == 'p0':
            _finish(S, block); s_scope.close(); return nc

        p1 = ExitStack()
        with p1:
            wk_h = SB(p1, "wk_h", [128, ND * D], BF16); wk_l = SB(p1, "wk_l", [128, ND * D], BF16)
            wv_h = SB(p1, "wv_h", [128, ND * D], BF16); wv_l = SB(p1, "wv_l", [128, ND * D], BF16)
            ws_scope = ExitStack()
            wstage = SB(ws_scope, "wstage", [128, ND * D])

            hws = h_p0_done
            for (wd, wh, wl) in [(WT_d["WkT"], wk_h, wk_l), (WT_d["WvT"], wv_h, wv_l)]:
                def ldw(e, dma, wd=wd):
                    dma(out=wstage[:].rearrange("p (dt o) -> p dt o", dt=ND),
                        in_=wd.rearrange("(dt p) o -> p dt o", p=128))
                hlw = S.step("sp", hws if isinstance(hws, list) else [hws], ldw, n_dma=1)

                def splw(v, wh=wh, wl=wl):
                    v.tensor_copy(wh[:], wstage[:])
                    return v.tensor_sub(wl[:], wstage[:], wh[:])
                hws = S.step("dve", [hlw], splw)
            ws_scope.close()
            h_wkv = hws

            xT_st = SB(p1, "xT_st", [128, ND * GT])
            xh = SB(p1, "xh", [128, ND * GT], BF16); xl = SB(p1, "xl", [128, ND * GT], BF16)
            kD_f = SB(p1, "kD_f", [128, ND * GT])
            kT_f = SB(p1, "kT_f", [128, G * D])
            kT_h = SB(p1, "kT_h", [128, G * D], BF16); kT_l = SB(p1, "kT_l", [128, G * D], BF16)
            vT_f = SB(p1, "vT_f", [128, 512])
            vT_h = SB(p1, "vT_h", [128, G * D], BF16); vT_l = SB(p1, "vT_l", [128, G * D], BF16)
            ztmp = SB(p1, "ztmp", [128, 1])
            pk = PS(p1, "pk", [128, 2 * GT])
            pv = PS(p1, "pv", [128, D])
            pkv = PS(p1, "pkv", [128, 2 * 512])
            ptr1 = PS(p1, "ptr1", [128, 128])

            prev_split = h_wkv
            prev_x_pe = None
            prev_kv_done = h_init
            prev_spk = None; prev_spv = None
            prev_kv_pe = None
            for g in range(NG):
                def ldx(e, dma, g=g):
                    dma(out=xT_st[:].rearrange("p (dt t) -> p dt t", dt=ND),
                        in_=xT_d[:, g * GT:(g + 1) * GT]
                        .rearrange("(dt p) t -> p dt t", p=128))
                hx = S.step("sp", [prev_split], ldx, n_dma=1)

                def splx(v):
                    v.tensor_copy(xh[:], xT_st[:])
                    return v.tensor_sub(xl[:], xT_st[:], xh[:])
                hxs = S.step("dve", [hx, h_wkv, prev_x_pe], splx)

                def spz(e, dma, g=g):
                    dma(out=prefix_z[g], in_=z_acc[:])
                    dma(out=prefix_s[g], in_=Smaster[:])
                hspill = S.step("gp", [prev_kv_done], spz, n_dma=2)

                hk_evac = []
                for ot in range(ND):
                    def mmk(p, ot=ot):
                        pairs = []
                        for dk in range(ND):
                            lh = wk_h[:, dk * D + ot * 128: dk * D + ot * 128 + 128]
                            ll = wk_l[:, dk * D + ot * 128: dk * D + ot * 128 + 128]
                            rh = xh[:, dk * GT:(dk + 1) * GT]
                            rl = xl[:, dk * GT:(dk + 1) * GT]
                            pairs += hl3(lh, ll, rh, rl)
                        return mm_acc(p, pk[:, (ot % 2) * GT:(ot % 2 + 1) * GT], pairs)
                    dep = [hxs] + ([hk_evac[ot - 2][1]] if ot >= 2 else [])
                    hmk = S.step("pe", dep, mmk)

                    def ek_act(a, ot=ot):
                        return a.activation(kT_f[:, 0:GT], pk[:, (ot % 2) * GT:(ot % 2 + 1) * GT],
                                            ACT.Exp, bias=bk_s[:, ot:ot + 1])
                    hek = S.step("act", [hmk, prev_spk, prev_kv_pe]
                                 + ([hk_evac[ot - 1][1]] if ot >= 1 else []),
                                 ek_act)

                    def ek_dve(v, ot=ot):
                        v.tensor_scalar(kT_f[:, GT:2 * GT], pk[:, (ot % 2) * GT:(ot % 2 + 1) * GT],
                                        bk_s[:, ot:ot + 1], 0.0, ALU.add, ALU.max)
                        v.scalar_tensor_tensor(kD_f[:, ot * GT:(ot + 1) * GT],
                                               kT_f[:, 0:GT], 1.0, kT_f[:, GT:2 * GT],
                                               ALU.min, ALU.add)
                        v.reduce_sum(ztmp[:], kD_f[:, ot * GT:(ot + 1) * GT], axis=AX.X)
                        return v.tensor_add(z_acc[:, ot:ot + 1], z_acc[:, ot:ot + 1], ztmp[:])
                    hev = S.step("dve", [hmk, hek, hspill, prev_x_pe, prev_spk], ek_dve)
                    hk_evac.append((hmk, hev))

                htr_prev_ev = None
                for tc in range(G):
                    for dt in range(ND):
                        def trk(p, tc=tc, dt=dt):
                            return p.transpose(
                                ptr1[:],
                                kD_f[:, dt * GT + tc * 128: dt * GT + tc * 128 + 128],
                                ident_f[:])
                        dep = [hk_evac[dt][1]] + ([htr_prev_ev] if htr_prev_ev else [])
                        htk = S.step("pe", dep, trk)

                        def evtr(v, tc=tc, dt=dt):
                            return v.tensor_copy(
                                kT_f[:, tc * D + dt * 128: tc * D + dt * 128 + 128],
                                ptr1[:])
                        htr_prev_ev = S.step("dve", [htk], evtr)

                def splkT(v):
                    v.tensor_copy(kT_h[:], kT_f[:])
                    return v.tensor_sub(kT_l[:], kT_f[:], kT_h[:])
                hkT = S.step("dve", [htr_prev_ev, prev_spk, prev_kv_pe], splkT)

                def spk(e, dma, g=g):
                    dma(out=spill_k[g, 0], in_=kT_h[:])
                    dma(out=spill_k[g, 1], in_=kT_l[:])
                hspk = S.step("gp", [hkT], spk, n_dma=2)

                hv_evac = []
                for tt in range(G):
                    hmv_eh = None
                    for eh in range(2):
                        def mmv(p, tt=tt, eh=eh):
                            pairs = []
                            for dk in range(ND):
                                lh = xh[:, dk * GT + tt * 128: dk * GT + tt * 128 + 128]
                                ll = xl[:, dk * GT + tt * 128: dk * GT + tt * 128 + 128]
                                rh = wv_h[:, dk * D + eh * 512: dk * D + eh * 512 + 512]
                                rl = wv_l[:, dk * D + eh * 512: dk * D + eh * 512 + 512]
                                pairs += hl3(lh, ll, rh, rl)
                            return mm_acc(p, pv[:, eh * 512:(eh + 1) * 512], pairs)
                        dep = [hxs] + ([hv_evac[tt - 1][1]] if tt >= 1 else [])
                        hmv_eh = S.step("pe", dep, mmv)

                    def ev_v(v, tt=tt):
                        last = None
                        for eh in range(2):
                            dst = slice(tt * D + eh * 512, tt * D + eh * 512 + 512)
                            v.scalar_tensor_tensor(vT_f[:], pv[:, eh * 512:(eh + 1) * 512],
                                                   0.0, bvb_s[:, eh * 512:(eh + 1) * 512],
                                                   ALU.add, ALU.add)
                            v.tensor_copy(vT_h[:, dst], vT_f[:])
                            last = v.tensor_sub(vT_l[:, dst], vT_f[:], vT_h[:, dst])
                        return last
                    hve = S.step("dve", [hmv_eh, prev_spv, prev_kv_pe], ev_v)
                    hv_evac.append((hmv_eh, hve))

                def spv(e, dma, g=g):
                    dma(out=spill_v[g, 0], in_=vT_h[:])
                    dma(out=spill_v[g, 1], in_=vT_l[:])
                hspv = S.step("gp", [hv_evac[-1][1]], spv, n_dma=2)

                hkv_ev = []
                hm_last = None
                for dt in range(ND):
                    for eh in range(2):
                        idx = dt * 2 + eh

                        def mmkv(p, dt=dt, eh=eh, idx=idx):
                            pairs = []
                            for tc in range(G):
                                lh = kT_h[:, tc * D + dt * 128: tc * D + dt * 128 + 128]
                                ll = kT_l[:, tc * D + dt * 128: tc * D + dt * 128 + 128]
                                rh = vT_h[:, tc * D + eh * 512: tc * D + eh * 512 + 512]
                                rl = vT_l[:, tc * D + eh * 512: tc * D + eh * 512 + 512]
                                pairs += hl3(lh, ll, rh, rl)
                            slot = idx % 2
                            return mm_acc(p, pkv[:, slot * 512:(slot + 1) * 512], pairs)
                        dep = [hkT, hv_evac[-1][1]]
                        if idx >= 2:
                            dep.append(hkv_ev[idx - 2])
                        hm_last = S.step("pe", dep, mmkv)

                        def evkv(v, dt=dt, eh=eh, idx=idx):
                            slot = idx % 2
                            dst = Smaster[:, dt * D + eh * 512: dt * D + eh * 512 + 512]
                            return v.tensor_add(dst, dst, pkv[:, slot * 512:(slot + 1) * 512])
                        hkv_ev.append(S.step("dve", [hm_last, hspill], evkv))

                prev_kv_done = hkv_ev[-1]
                prev_kv_pe = hm_last
                prev_x_pe = hm_last
                prev_split = hxs
                prev_spk = hspk; prev_spv = hspv

            h_p1a_done = S.all_done()

        if upto == 'p1a':
            _finish(S, block); s_scope.close(); return nc

        def wr_cc(e, dma):
            for dt in range(ND):
                dma(out=cc_in[dt * 128:(dt + 1) * 128, :],
                    in_=Smaster[:, dt * D:(dt + 1) * D])
            for dt in range(ND):
                dma(out=cc_in[D:D + 1, dt * 128:(dt + 1) * 128],
                    in_=z_acc[:, dt:dt + 1])
        h_wrcc = S.step("gp", h_p1a_done, wr_cc, n_dma=16)

        _wr_val = h_wrcc[1]

        def _cc_emit(e):
            e.wait_ge(S.sem["gp"], _wr_val)
            e.collective_compute(
                "AllGather", ALU.bypass, replica_groups=REPLICA_GROUPS,
                ins=[cc_in[:]], outs=[cc_out[:]]).then_inc(S.sem["cc"], 1)
        block.gpsimd(_cc_emit)
        S.cnt["cc"] = 1
        h_cc = ("cc", 1)

        p1b = ExitStack()
        with p1b:
            wq_h = SB(p1b, "wq_h", [128, ND * D], BF16); wq_l = SB(p1b, "wq_l", [128, ND * D], BF16)
            ws2_scope = ExitStack()
            wstage2 = SB(ws2_scope, "wstage2", [128, ND * D])

            def ldwq(e, dma):
                dma(out=wstage2[:].rearrange("p (dt o) -> p dt o", dt=ND),
                    in_=WT_d["WqT"].rearrange("(dt p) o -> p dt o", p=128))
            hlq = S.step("sp", h_p1a_done, ldwq, n_dma=1)

            def splwq(v):
                v.tensor_copy(wq_h[:], wstage2[:])
                return v.tensor_sub(wq_l[:], wstage2[:], wq_h[:])
            hwq = S.step("dve", [hlq], splwq)
            ws2_scope.close()

            xT_st2 = SB(p1b, "xT_st2", [128, ND * GT])
            xh2 = SB(p1b, "xh2", [128, ND * GT], BF16); xl2 = SB(p1b, "xl2", [128, ND * GT], BF16)
            qD_f = SB(p1b, "qD_f", [128, ND * GT])
            q_h = SB(p1b, "q_h", [128, ND * GT], BF16); q_l = SB(p1b, "q_l", [128, ND * GT], BF16)
            ephi2 = SB(p1b, "ephi2", [128, GT]); rphi2 = SB(p1b, "rphi2", [128, GT])
            pq = PS(p1b, "pq", [128, 2 * GT])

            prev_split = hwq
            prev_x_pe = None
            prev_spq = None
            for g in range(NG):
                def ldx2(e, dma, g=g):
                    dma(out=xT_st2[:].rearrange("p (dt t) -> p dt t", dt=ND),
                        in_=xT_d[:, g * GT:(g + 1) * GT]
                        .rearrange("(dt p) t -> p dt t", p=128))
                hx = S.step("sp", [prev_split], ldx2, n_dma=1)

                def splx2(v):
                    v.tensor_copy(xh2[:], xT_st2[:])
                    return v.tensor_sub(xl2[:], xT_st2[:], xh2[:])
                hxs = S.step("dve", [hx, hwq, prev_x_pe], splx2)

                hq_evac = []
                hmq = None
                for ot in range(ND):
                    def mmq(p, ot=ot):
                        pairs = []
                        for dk in range(ND):
                            lh = wq_h[:, dk * D + ot * 128: dk * D + ot * 128 + 128]
                            ll = wq_l[:, dk * D + ot * 128: dk * D + ot * 128 + 128]
                            rh = xh2[:, dk * GT:(dk + 1) * GT]
                            rl = xl2[:, dk * GT:(dk + 1) * GT]
                            pairs += hl3(lh, ll, rh, rl)
                        return mm_acc(p, pq[:, (ot % 2) * GT:(ot % 2 + 1) * GT], pairs)
                    dep = [hxs] + ([hq_evac[ot - 2][1]] if ot >= 2 else [])
                    hmq = S.step("pe", dep, mmq)

                    def eq_act(a, ot=ot):
                        return a.activation(ephi2[:], pq[:, (ot % 2) * GT:(ot % 2 + 1) * GT],
                                            ACT.Exp, bias=bq_s[:, ot:ot + 1])
                    heq = S.step("act", [hmq] + ([hq_evac[ot - 1][1]] if ot >= 1 else []),
                                 eq_act)

                    def eq_dve(v, ot=ot):
                        v.tensor_scalar(rphi2[:], pq[:, (ot % 2) * GT:(ot % 2 + 1) * GT],
                                        bq_s[:, ot:ot + 1], 0.0, ALU.add, ALU.max)
                        v.scalar_tensor_tensor(qD_f[:, ot * GT:(ot + 1) * GT],
                                               ephi2[:], 1.0, rphi2[:], ALU.min, ALU.add)
                        v.tensor_copy(q_h[:, ot * GT:(ot + 1) * GT],
                                      qD_f[:, ot * GT:(ot + 1) * GT])
                        return v.tensor_sub(q_l[:, ot * GT:(ot + 1) * GT],
                                            qD_f[:, ot * GT:(ot + 1) * GT],
                                            q_h[:, ot * GT:(ot + 1) * GT])
                    hev = S.step("dve", [hmq, heq, prev_spq, prev_x_pe], eq_dve)
                    hq_evac.append((hmq, hev))

                def spq(e, dma, g=g):
                    dma(out=spill_q[g, 0], in_=q_h[:])
                    dma(out=spill_q[g, 1], in_=q_l[:])
                prev_spq = S.step("gp", [hq_evac[-1][1]], spq, n_dma=2)
                prev_split = hxs
                prev_x_pe = hmq

            h_p1b_done = S.all_done()

        if upto == 'p1b':
            _finish(S, block); s_scope.close(); return nc

        px = ExitStack()
        with px:
            Ttmp = SB(px, "Ttmp", [128, ND * D])
            ztmp8 = SB(px, "ztmp8", [128, ND])
            hacc = None
            for r in range(4):
                def ldT(e, dma, r=r):
                    base = r * (D + 1)
                    dma(out=Ttmp[:].rearrange("p (dt e) -> p dt e", dt=ND),
                        in_=cc_out[base:base + D, :].rearrange("(dt p) e -> p dt e", p=128))
                    dma(out=ztmp8[:],
                        in_=cc_out[base + D:base + D + 1, :]
                        .rearrange("r (dt p) -> p (r dt)", p=128, r=1),
                        allow_slow_non_contiguous=True)
                dep = [h_cc] + ([hacc] if hacc else []) + (h_p1b_done if r == 0 else [])
                hl = S.step("sp", dep, ldT, n_dma=2)

                def accT(v, r=r):
                    if r == 0:
                        v.tensor_scalar(Smaster[:], Ttmp[:], pmask_s[:, 0:1], None, ALU.mult)
                        return v.tensor_scalar(z_start[:], ztmp8[:], pmask_s[:, 0:1],
                                               None, ALU.mult)
                    v.scalar_tensor_tensor(Smaster[:], Ttmp[:], pmask_s[:, r:r + 1],
                                           Smaster[:], ALU.mult, ALU.add)
                    return v.scalar_tensor_tensor(z_start[:], ztmp8[:], pmask_s[:, r:r + 1],
                                                  z_start[:], ALU.mult, ALU.add)
                hacc = S.step("dve", [hl] + h_p1b_done, accT)

            def spss(e, dma):
                dma(out=sstart_d[:], in_=Smaster[:])
            h_spss = S.step("gp", [hacc], spss, n_dma=1)
            h_x2_done = S.all_done()
        s_scope.close()
        if upto == 'x2':
            _finish(S, block); return nc

        p2 = ExitStack()
        with p2:
            wo_h = SB(p2, "wo_h", [128, ND * D], BF16); wo_l = SB(p2, "wo_l", [128, ND * D], BF16)
            ws3_scope = ExitStack()
            wstage3 = SB(ws3_scope, "wstage3", [128, ND * D])

            def ldwo(e, dma):
                dma(out=wstage3[:].rearrange("p (dt o) -> p dt o", dt=ND),
                    in_=WT_d["WoT"].rearrange("(dt p) o -> p dt o", p=128))
            hlo = S.step("sp", h_x2_done, ldwo, n_dma=1)

            def splwo(v):
                v.tensor_copy(wo_h[:], wstage3[:])
                return v.tensor_sub(wo_l[:], wstage3[:], wo_h[:])
            hwo = S.step("dve", [hlo], splwo)
            ws3_scope.close()
            if p2sub == 'w':
                _finish(S, block)
                return nc

            Sgrp = SB(p2, "Sgrp", [128, ND * D])
            s_h = SB(p2, "s_h", [128, ND * D], BF16); s_l = SB(p2, "s_l", [128, ND * D], BF16)
            q2_h = SB(p2, "q2_h", [128, ND * GT], BF16); q2_l = SB(p2, "q2_l", [128, ND * GT], BF16)
            k2_h = SB(p2, "k2_h", [128, G * D], BF16); k2_l = SB(p2, "k2_l", [128, G * D], BF16)
            v2_h = SB(p2, "v2_h", [128, G * D], BF16); v2_l = SB(p2, "v2_l", [128, G * D], BF16)
            kd_h = SB(p2, "kd_h", [128, ND * 128], BF16); kd_l = SB(p2, "kd_l", [128, ND * 128], BF16)
            aT_h = SB(p2, "aT_h", [128, G * GT], BF16); aT_l = SB(p2, "aT_l", [128, G * GT], BF16)
            n_h = SB(p2, "n_h", [128, ND * GT], BF16); n_l = SB(p2, "n_l", [128, ND * GT], BF16)
            atmp = SB(p2, "atmp", [128, 128])
            dcol = SB(p2, "dcol", [128, G])
            recip_den = SB(p2, "recip_den", [128, G])
            x_res = SB(p2, "x_res", [128, D])
            y_t = SB(p2, "y_t", [128, D])
            y1_t = SB(p2, "y1_t", [128, D])
            zgtmp = SB(p2, "zgtmp", [128, ND])
            bnb = SB(p2, "bnb", [128, 12]); bna = SB(p2, "bna", [128, 2])
            sd_c = SB(p2, "sd_c", [128, 1]); rstd_c = SB(p2, "rstd_c", [128, 1])
            negm_c = SB(p2, "negm_c", [128, 1])
            pA = PS(p2, "pA", [128, 512])
            pden = PS(p2, "pden", [128, G])
            pnum2 = PS(p2, "pnum2", [128, 2 * 512])
            pwo = PS(p2, "pwo", [128, D])
            ptrbA = PS(p2, "ptrbA", [128, 512], BF16)
            ptrbB = PS(p2, "ptrbB", [128, 512], BF16)

            h_prev_group_end = None
            prev_hey = None
            prev_hsd = None
            prev_sty1 = None
            prev_hxr_reader = None
            prev_trev = None
            prev_mn_pe = None
            h_ldsz = None
            h_accS = None
            for g in range(NG if not p2sub else 1):
              try:
                  if h_ldsz is None:
                      def ldsz0(e, dma, g=g):
                          dma(out=Sgrp[:], in_=prefix_s[g])
                          dma(out=zgtmp[:], in_=prefix_z[g])
                      h_ldsz = S.step("sp", h_x2_done + [hwo], ldsz0, n_dma=2)

                      def accS0(e, dma):
                          for i in range(4):
                              dma(out=Sgrp[:, i * 2048:(i + 1) * 2048],
                                  in_=sstart_d[:, i * 2048:(i + 1) * 2048],
                                  accum_op=ALU.add)
                      h_accS = S.step("gp", [h_ldsz], accS0, n_dma=4)

                  def ldkqv(e, dma, g=g):
                      dma(out=q2_h[:], in_=spill_q[g, 0]); dma(out=q2_l[:], in_=spill_q[g, 1])
                      dma(out=k2_h[:], in_=spill_k[g, 0]); dma(out=k2_l[:], in_=spill_k[g, 1])
                      dma(out=v2_h[:], in_=spill_v[g, 0]); dma(out=v2_l[:], in_=spill_v[g, 1])
                  hld = S.step("sp", [prev_mn_pe], ldkqv, n_dma=6)
                  if p2sub == 'ld0':
                      raise _StopEmit

                  hz0 = S.step("dve", [h_ldsz, prev_mn_pe], lambda v: v.tensor_add(
                      zg[:], zgtmp[:], z_start[:]))

                  def mks_hi(a):
                      a.copy(s_h[:], Sgrp[:])
                      return a.copy(zg_h[:], zg[:])
                  hSa = S.step("act", [h_ldsz, h_accS, hz0, prev_mn_pe], mks_hi)

                  def mks_lo(v):
                      v.tensor_sub(s_l[:], Sgrp[:], s_h[:])
                      return v.tensor_sub(zg_l[:], zg[:], zg_h[:])
                  hS = S.step("dve", [hSa], mks_lo)
                  if p2sub == 'ld':
                      raise _StopEmit

                  if g + 1 < NG:
                      def ldsz(e, dma, g=g):
                          dma(out=Sgrp[:], in_=prefix_s[g + 1])
                          dma(out=zgtmp[:], in_=prefix_z[g + 1])
                      h_ldsz = S.step("sp", [hS], ldsz, n_dma=2)

                      def accSn(e, dma):
                          for i in range(4):
                              dma(out=Sgrp[:, i * 2048:(i + 1) * 2048],
                                  in_=sstart_d[:, i * 2048:(i + 1) * 2048],
                                  accum_op=ALU.add)
                      h_accS = S.step("gp", [h_ldsz], accSn, n_dma=4)

                  hA_ev = []
                  for st_ in range(G):
                      htr_ev = None
                      for (ksrc, kdst) in ((k2_h, kd_h), (k2_l, kd_l)):
                          for half in range(2):
                              ptrb = ptrbA if half == 0 else ptrbB

                              def trk2(p, st_=st_, half=half, ksrc=ksrc, ptrb=ptrb):
                                  last = None
                                  for j in range(4):
                                      dk = half * 4 + j
                                      last = p.transpose(
                                          ptrb[:, j * 128:(j + 1) * 128],
                                          ksrc[:, st_ * D + dk * 128: st_ * D + dk * 128 + 128],
                                          ident_b[:])
                                  return last
                              dep = [hld, prev_trev]
                              if hA_ev:
                                  dep.append(hA_ev[-1])
                              ht = S.step("pe", dep, trk2)

                              def evt(a, half=half, kdst=kdst, ptrb=ptrb):
                                  return a.copy(kdst[:, half * 512:(half + 1) * 512],
                                                ptrb[:])
                              prev_trev = S.step("act", [ht], evt)
                          htr_ev = prev_trev

                      ncols = GT - st_ * 128

                      def mma(p, st_=st_, ncols=ncols):
                          pairs = []
                          for dk in range(ND):
                              lh = kd_h[:, dk * 128:(dk + 1) * 128]
                              ll = kd_l[:, dk * 128:(dk + 1) * 128]
                              rh = q2_h[:, dk * GT + st_ * 128: dk * GT + GT]
                              rl = q2_l[:, dk * GT + st_ * 128: dk * GT + GT]
                              pairs += hl3(lh, ll, rh, rl)
                          return mm_acc(p, pA[:, 0:ncols], pairs)
                      dep = [htr_ev, hS]
                      if hA_ev:
                          dep.append(hA_ev[-1])
                      hma = S.step("pe", dep, mma)

                      def eva(v, st_=st_, ncols=ncols):
                          base = st_ * GT + st_ * 128
                          v.tensor_mul(atmp[:], pA[:, 0:128], triu_f[:])
                          v.tensor_copy(aT_h[:, base:base + 128], atmp[:])
                          v.scalar_tensor_tensor(aT_l[:, base:base + 128], atmp[:],
                                                 1.0, aT_h[:, base:base + 128],
                                                 ALU.mult, ALU.subtract)
                          if ncols > 128:
                              v.tensor_copy(aT_h[:, base + 128: st_ * GT + GT],
                                            pA[:, 128:ncols])
                              return v.scalar_tensor_tensor(
                                  aT_l[:, base + 128: st_ * GT + GT], pA[:, 128:ncols],
                                  1.0, aT_h[:, base + 128: st_ * GT + GT],
                                  ALU.mult, ALU.subtract)
                          return v.engine_nop()
                      hA_ev.append(S.step("dve", [hma], eva))
                  if p2sub == 'A':
                      raise _StopEmit

                  def mden(p):
                      last = None
                      first = True
                      for cq in range(G):
                          out = pden[:, cq:cq + 1]
                          for st_ in range(cq + 1):
                              base = st_ * GT + cq * 128
                              for pl in (aT_h, aT_l):
                                  last = p.matmul(out, pl[:, base:base + 128],
                                                  ones_bf[:, 0:1],
                                                  start=first, stop=False)
                                  first = False
                          for dk in range(ND):
                              qh_sl = q2_h[:, dk * GT + cq * 128: dk * GT + cq * 128 + 128]
                              ql_sl = q2_l[:, dk * GT + cq * 128: dk * GT + cq * 128 + 128]
                              for (qp, zp) in ((qh_sl, zg_h), (ql_sl, zg_h), (qh_sl, zg_l)):
                                  isl = (cq == G - 1 and dk == ND - 1 and
                                         qp is qh_sl and zp is zg_l)
                                  last = p.matmul(out, qp, zp[:, dk:dk + 1],
                                                  start=False, stop=isl)
                      return last
                  hmden = S.step("pe", hA_ev + [hS], mden)

                  def evden(v):
                      v.tensor_scalar(dcol[:], pden[:], 1e-6, None, ALU.add)
                      return v.reciprocal(recip_den[:], dcol[:])
                  hevden = S.step("dve", [hmden], evden)
                  if p2sub == 'den':
                      raise _StopEmit

                  h_ev_lo = {}
                  h_ev_hi = {}
                  hmn = None
                  for et in range(ND):
                      def mnum(p, et=et):
                          slot = et % 2
                          pairs = []
                          for dk in range(ND):
                              sh_sl = s_h[:, dk * D + et * 128: dk * D + et * 128 + 128]
                              sl_sl = s_l[:, dk * D + et * 128: dk * D + et * 128 + 128]
                              qh_sl = q2_h[:, dk * GT:(dk + 1) * GT]
                              ql_sl = q2_l[:, dk * GT:(dk + 1) * GT]
                              pairs += hl3(sh_sl, sl_sl, qh_sl, ql_sl)
                          n = len(pairs)
                          last = None
                          for i, (l, r) in enumerate(pairs):
                              last = p.matmul(pnum2[:, slot * 512:(slot + 1) * 512],
                                              l, r, start=(i == 0), stop=False)
                          for st_ in range(G):
                              vh_sl = v2_h[:, st_ * D + et * 128: st_ * D + et * 128 + 128]
                              vl_sl = v2_l[:, st_ * D + et * 128: st_ * D + et * 128 + 128]
                              ah_sl = aT_h[:, st_ * GT + st_ * 128: st_ * GT + GT]
                              al_sl = aT_l[:, st_ * GT + st_ * 128: st_ * GT + GT]
                              ipairs = [(vh_sl, ah_sl), (vh_sl, al_sl), (vl_sl, ah_sl)]
                              for j, (l, r) in enumerate(ipairs):
                                  isl = (st_ == G - 1 and j == 2)
                                  last = p.matmul(
                                      pnum2[:, slot * 512 + st_ * 128:(slot + 1) * 512],
                                      l, r, start=False, stop=isl)
                          return last
                      dep = [hS, hA_ev[-1]]
                      if et >= 2:
                          dep += [h_ev_lo[et - 2], h_ev_hi[et - 2]]
                      hmn = S.step("pe", dep, mnum)

                      def ev_hi(a, et=et):
                          return a.copy(n_h[:, et * GT:(et + 1) * GT],
                                        pnum2[:, (et % 2) * 512:(et % 2 + 1) * 512])
                      h_ev_hi[et] = S.step("act", [hmn], ev_hi)

                      def ev_lo(v, et=et):
                          return v.scalar_tensor_tensor(
                              n_l[:, et * GT:(et + 1) * GT],
                              pnum2[:, (et % 2) * 512:(et % 2 + 1) * 512],
                              1.0, n_h[:, et * GT:(et + 1) * GT],
                              ALU.mult, ALU.subtract)
                      h_ev_lo[et] = S.step("dve", [hmn, h_ev_hi[et]], ev_lo)
                  prev_mn_pe = hmn
                  if p2sub == 'num':
                      raise _StopEmit

                  for cq in range(G):
                      def mwo(p, cq=cq):
                          last = None
                          for eh in range(2):
                              pairs = []
                              for ek in range(ND):
                                  lh = n_h[:, ek * GT + cq * 128: ek * GT + cq * 128 + 128]
                                  ll = n_l[:, ek * GT + cq * 128: ek * GT + cq * 128 + 128]
                                  rh = wo_h[:, ek * D + eh * 512: ek * D + eh * 512 + 512]
                                  rl = wo_l[:, ek * D + eh * 512: ek * D + eh * 512 + 512]
                                  pairs += hl3(lh, ll, rh, rl)
                              last = mm_acc(p, pwo[:, eh * 512:(eh + 1) * 512], pairs)
                          return last
                      hmwo = S.step("pe", [h_ev_lo[ND - 1], h_ev_hi[ND - 1], prev_hey], mwo)
                      if p2sub == 'wo':
                          raise _StopEmit

                      def ldxr(e, dma, g=g, cq=cq):
                          t0 = (g * G + cq) * 128
                          dma(out=x_res[:], in_=x_d[t0:t0 + 128, :])
                      hxr = S.step("sp", [prev_hxr_reader], ldxr, n_dma=1)

                      def evy(v, cq=cq):
                          v.scalar_tensor_tensor(y_t[:], pwo[:], recip_den[:, cq:cq + 1],
                                                 bob_s[:], ALU.mult, ALU.add)
                          v.tensor_add(y_t[:], y_t[:], x_res[:])
                          v.bn_stats(bnb[:, 0:6], y_t[:, 0:512])
                          v.bn_stats(bnb[:, 6:12], y_t[:, 512:1024])
                          v.bn_aggr(bna[:, 0:2], bnb[:, 0:12])
                          return v.tensor_scalar(negm_c[:], bna[:, 0:1], -1.0, None, ALU.mult)
                      hey = S.step("dve", [hmwo, hxr, hevden, prev_hsd], evy)
                      prev_hey = hey
                      prev_hxr_reader = hey

                      hsd = S.step("act", [hey], lambda a: a.activation(
                          sd_c[:], bna[:, 1:2], ACT.Sqrt, bias=eps_c[:, 0:1]))
                      prev_hsd = hsd

                      def evy2(v):
                          v.reciprocal(rstd_c[:], sd_c[:])
                          return v.tensor_scalar(y1_t[:], y_t[:], negm_c[:, 0:1],
                                                 rstd_c[:, 0:1], ALU.add, ALU.mult)
                      hy1 = S.step("dve", [hsd, prev_sty1], evy2)

                      def sty1(e, dma, g=g, cq=cq):
                          t0 = (g * G + cq) * 128
                          dma(out=y1_dram[t0:t0 + 128, :], in_=y1_t[:])
                      prev_sty1 = S.step("gp", [hy1], sty1, n_dma=1)

                  h_prev_group_end = hy1
              except _StopEmit:
                pass

            h_p2_done = S.all_done()
            if p2sub:
                pass

        if upto == 'p2':
            _finish(S, block); return nc

        p3a = ExitStack()
        with p3a:
            w1q_s = SB(p3a, "w1q_s", [128, ND * H], BF16)
            y1_ld = SB(p3a, "y1_ld", [128, D])
            m1c = SB(p3a, "m1c", [128, 1]); r1c = SB(p3a, "r1c", [128, 1])
            f1c = SB(p3a, "f1c", [128, 1])
            m2c = SB(p3a, "m2c", [128, 1]); r2c = SB(p3a, "r2c", [128, 1])
            mtmp = SB(p3a, "mtmp", [128, 1])
            xi_ = SB(p3a, "xi_", [128, D], I32)
            xb_ = SB(p3a, "xb_", [128, D], BF16)
            yqD = SB(p3a, "yqD", [128, ND * 128], BF16)
            h_f = SB(p3a, "h_f", [128, H])
            hi_ = SB(p3a, "hi_", [128, H], I32)
            hb_ = SB(p3a, "hb_", [128, H], BF16)
            ph_ = PS(p3a, "ph_", [128, 2 * 512])
            ptr3 = PS(p3a, "ptr3", [128, 128], BF16)

            hw1 = S.step("sp", h_p2_done, lambda e, dma: dma(out=w1q_s[:], in_=w1q_dram[:])
                         and None, n_dma=1)

            prev_hq2 = None
            prev_trq_pe = None
            prev_gel_last = None
            prev_sth = None
            prev_hly_gate = hw1
            for ti in range(NT):
                def ldy(e, dma, ti=ti):
                    dma(out=y1_ld[:], in_=y1_dram[ti * 128:(ti + 1) * 128, :])
                hly = S.step("sp", [prev_hly_gate], ldy, n_dma=1)

                def q1(v):
                    v.reduce_max(m1c[:], y1_ld[:], axis=AX.X, apply_absolute_value=True)
                    v.tensor_scalar(m1c[:], m1c[:], 1e-5, None, ALU.max)
                    v.reciprocal(r1c[:], m1c[:])
                    v.tensor_scalar(f1c[:], m1c[:], s1bc[:, 0:1], 1.0 / 127.0,
                                    ALU.mult, ALU.mult)
                    v.tensor_scalar(xi_[:], y1_ld[:], r1c[:, 0:1], 127.0,
                                    ALU.mult, ALU.mult)
                    return v.tensor_copy(xb_[:], xi_[:])
                hq1 = S.step("dve", [hly, prev_trq_pe, prev_gel_last], q1)

                htr_ev = None
                htr_pe = None
                for dt in range(ND):
                    def trq(p, dt=dt):
                        return p.transpose(ptr3[:], xb_[:, dt * 128:(dt + 1) * 128],
                                           ident_b[:])
                    dep = [hq1] + ([htr_ev] if htr_ev else [])
                    htr_pe = S.step("pe", dep, trq)

                    def evq(v, dt=dt):
                        return v.tensor_copy(yqD[:, dt * 128:(dt + 1) * 128], ptr3[:])
                    htr_ev = S.step("dve", [htr_pe], evq)
                prev_trq_pe = htr_pe

                hml = []
                for hh in range(8):
                    def mm1(p, hh=hh):
                        pairs = [(yqD[:, dk * 128:(dk + 1) * 128],
                                  w1q_s[:, dk * H + hh * 512: dk * H + hh * 512 + 512])
                                 for dk in range(ND)]
                        return mm_acc(p, ph_[:, (hh % 2) * 512:(hh % 2 + 1) * 512], pairs)
                    dep = [htr_ev, hw1] + ([hml[-2][1]] if len(hml) >= 2 else [])
                    hm = S.step("pe", dep, mm1)

                    def gel(a, hh=hh):
                        return a.activation(h_f[:, hh * 512:(hh + 1) * 512],
                                            ph_[:, (hh % 2) * 512:(hh % 2 + 1) * 512],
                                            ACT.Gelu, scale=f1c[:, 0:1])
                    gdep = [hm] + ([prev_hq2] if (hh == 0 and prev_hq2) else [])
                    hg = S.step("act", gdep, gel)

                    def mx(v, hh=hh):
                        if hh == 0:
                            return v.reduce_max(m2c[:], h_f[:, 0:512], axis=AX.X,
                                                apply_absolute_value=True)
                        v.reduce_max(mtmp[:], h_f[:, hh * 512:(hh + 1) * 512], axis=AX.X,
                                     apply_absolute_value=True)
                        return v.tensor_max(m2c[:], m2c[:], mtmp[:])
                    hx_ = S.step("dve", [hg], mx)
                    hml.append((hm, hg, hx_))
                prev_gel_last = hml[-1][1]

                def q2(v, ti=ti):
                    v.tensor_scalar(m2c[:], m2c[:], 1e-5, None, ALU.max)
                    v.reciprocal(r2c[:], m2c[:])
                    v.tensor_scalar(f2all[:, ti:ti + 1], m2c[:], s2bc[:, 0:1], 1.0 / 127.0,
                                    ALU.mult, ALU.mult)
                    v.tensor_scalar(hi_[:], h_f[:], r2c[:, 0:1], 127.0, ALU.mult, ALU.mult)
                    return v.tensor_copy(hb_[:], hi_[:])
                hq2 = S.step("dve", [hml[-1][2], prev_sth], q2)
                prev_hq2 = hq2

                def sth(e, dma, ti=ti):
                    dma(out=hq_dram[ti], in_=hb_[:])
                prev_sth = S.step("gp", [hq2], sth, n_dma=1)
                prev_hly_gate = hq1

            h_p3a_done = S.all_done()

        if upto == 'p3a':
            _finish(S, block); return nc

        p3b = ExitStack()
        with p3b:
            w2q_s = SB(p3b, "w2q_s", [128, NH * D], BF16)
            hb2 = SB(p3b, "hb2", [128, H], BF16)
            y1_ld2 = SB(p3b, "y1_ld2", [128, D])
            hqD = SB(p3b, "hqD", [128, NH * 128], BF16)
            y2_t = SB(p3b, "y2_t", [128, D])
            out_t = SB(p3b, "out_t", [128, D])
            bnb2 = SB(p3b, "bnb2", [128, 12]); bna2 = SB(p3b, "bna2", [128, 2])
            sd2 = SB(p3b, "sd2", [128, 1]); rstd2 = SB(p3b, "rstd2", [128, 1])
            negm2 = SB(p3b, "negm2", [128, 1])
            po_ = PS(p3b, "po_", [128, D])
            ptr4 = PS(p3b, "ptr4", [128, 128], BF16)

            hw2 = S.step("sp", h_p3a_done, lambda e, dma: dma(out=w2q_s[:], in_=w2q_dram[:])
                         and None, n_dma=1)

            prev_hf = None
            prev_hm2 = None
            prev_hs2 = None
            prev_stout = None
            h_last_out = None
            for ti in range(NT):
                def ldh(e, dma, ti=ti):
                    dma(out=hb2[:], in_=hq_dram[ti])
                    dma(out=y1_ld2[:], in_=y1_dram[ti * 128:(ti + 1) * 128, :])
                dep = [hw2] if ti == 0 else [prev_hf, prev_hm2]
                hlh = S.step("sp", dep, ldh, n_dma=2)

                htr_ev = None
                for hk in range(NH):
                    def trh(p, hk=hk):
                        return p.transpose(ptr4[:], hb2[:, hk * 128:(hk + 1) * 128],
                                           ident_b[:])
                    dep = [hlh] + ([htr_ev] if htr_ev else [])
                    ht = S.step("pe", dep, trh)

                    def evh(v, hk=hk):
                        return v.tensor_copy(hqD[:, hk * 128:(hk + 1) * 128], ptr4[:])
                    htr_ev = S.step("dve", [ht], evh)

                def mm2(p):
                    last = None
                    for eh in range(2):
                        pairs = [(hqD[:, hk * 128:(hk + 1) * 128],
                                  w2q_s[:, hk * D + eh * 512: hk * D + eh * 512 + 512])
                                 for hk in range(NH)]
                        last = mm_acc(p, po_[:, eh * 512:(eh + 1) * 512], pairs)
                    return last
                hm2 = S.step("pe", [htr_ev, hw2], mm2)
                prev_hm2 = hm2

                def evy2b(v, ti=ti):
                    v.scalar_tensor_tensor(y2_t[:], po_[:], f2all[:, ti:ti + 1],
                                           y1_ld2[:], ALU.mult, ALU.add)
                    v.bn_stats(bnb2[:, 0:6], y2_t[:, 0:512])
                    v.bn_stats(bnb2[:, 6:12], y2_t[:, 512:1024])
                    v.bn_aggr(bna2[:, 0:2], bnb2[:, 0:12])
                    return v.tensor_scalar(negm2[:], bna2[:, 0:1], -1.0, None, ALU.mult)
                he2 = S.step("dve", [hm2, hlh, prev_hs2], evy2b)

                hs2 = S.step("act", [he2], lambda a: a.activation(
                    sd2[:], bna2[:, 1:2], ACT.Sqrt, bias=eps_c[:, 0:1]))
                prev_hs2 = hs2

                def fin(v):
                    v.reciprocal(rstd2[:], sd2[:])
                    return v.tensor_scalar(out_t[:], y2_t[:], negm2[:, 0:1],
                                           rstd2[:, 0:1], ALU.add, ALU.mult)
                hf = S.step("dve", [hs2, prev_stout], fin)
                prev_hf = hf

                def stout(e, dma, ti=ti):
                    dma(out=out_d[ti * 128:(ti + 1) * 128, :], in_=out_t[:])
                h_last_out = S.step("gp", [hf], stout, n_dma=1)
                prev_stout = h_last_out

            tgt = S.cnt["gp"]

            def gp_final(e):
                e.wait_ge(sem["gp"], tgt)
            block.gpsimd(gp_final)

    return nc


_CACHE = {}


def _get_nc(dbg=False):
    key = bool(dbg)
    if key not in _CACHE:
        _CACHE[key] = build_program(dbg=dbg)
    return _CACHE[key]


def _prep_inputs(inputs):
    x = np.asarray(inputs["L0_attn_x"], np.float32)
    Wq = np.asarray(inputs["L0_attn_Wq"], np.float32)
    bq = np.asarray(inputs["L0_attn_bq"], np.float32)
    Wk = np.asarray(inputs["L0_attn_Wk"], np.float32)
    bk = np.asarray(inputs["L0_attn_bk"], np.float32)
    Wv = np.asarray(inputs["L0_attn_Wv"], np.float32)
    bv = np.asarray(inputs["L0_attn_bv"], np.float32)
    Wo = np.asarray(inputs["L0_attn_Wo"], np.float32)
    bo = np.asarray(inputs["L0_attn_bo"], np.float32)
    w1 = np.asarray(inputs["L0_ffn_w1"], np.float32)
    w2 = np.asarray(inputs["L0_ffn_w2"], np.float32)

    shared = {
        "WqT": np.ascontiguousarray(Wq.T), "WkT": np.ascontiguousarray(Wk.T),
        "WvT": np.ascontiguousarray(Wv.T), "WoT": np.ascontiguousarray(Wo.T),
        "bq_t": np.ascontiguousarray(bq.reshape(ND, 128).T),
        "bk_t": np.ascontiguousarray(bk.reshape(ND, 128).T),
        "bv_b": np.ascontiguousarray(np.broadcast_to(bv, (128, D))),
        "bo_b": np.ascontiguousarray(np.broadcast_to(bo, (128, D))),
        "w1T": np.ascontiguousarray(w1.T), "w2T": np.ascontiguousarray(w2.T),
        "triu": np.triu(np.ones((128, 128), np.float32)),
        "ident": np.eye(128, dtype=np.float32),
    }
    in_maps = []
    for core in range(8):
        b, q = divmod(core, 4)
        seg = x[b, q * TSEG:(q + 1) * TSEG, :]
        m = dict(shared)
        m["x_seg"] = np.ascontiguousarray(seg)
        m["xT_seg"] = np.ascontiguousarray(seg.T)
        pm = (np.arange(4) < q).astype(np.float32)
        m["pmask"] = np.ascontiguousarray(np.broadcast_to(pm, (128, 4)))
        in_maps.append(m)
    return in_maps


def kernel(**inputs):
    in_maps = _prep_inputs(inputs)
    nc = _get_nc(dbg=False)
    res = run_bass_kernel_spmd(nc, in_maps, core_ids=list(range(8)))
    out = np.empty((B, T, D), np.float32)
    for core in range(8):
        b, q = divmod(core, 4)
        out[b, q * TSEG:(q + 1) * TSEG, :] = np.asarray(res.results[core]["out"])
    return out


# version 04
# speedup vs baseline: 1.1036x; 1.0116x over previous
import numpy as np
import ml_dtypes
from contextlib import ExitStack
import concourse.bass as bass
import concourse.mybir as mybir
from concourse.bass_utils import run_bass_kernel_spmd

F32 = mybir.dt.float32
BF16 = mybir.dt.bfloat16
I32 = mybir.dt.int32
AX = mybir.AxisListType
ALU = mybir.AluOpType
ACT = mybir.ActivationFunctionType

B, T, D = 2, 16384, 1024
H = 4 * D
TSEG = T // 4
C = 128
G = 4
GT = G * C
NG = TSEG // GT
ND = D // 128
NH = H // 128
NT = TSEG // C
REPLICA_GROUPS = [[0, 1, 2, 3], [4, 5, 6, 7]]

ENG_ATTR = {"pe": "tensor", "dve": "vector", "act": "scalar", "sp": "sync", "gp": "gpsimd"}


class _SerialEng:

    _PASSTHRU = {"wait_ge"}

    def __init__(self, eng, sem, state):
        self._eng = eng
        self._sem = sem
        self._state = state

    def __getattr__(self, name):
        fn = getattr(self._eng, name)
        if name in self._PASSTHRU or not callable(fn):
            return fn

        def wrapped(*a, **kw):
            st = self._state
            if st["issued"] > st["waited"]:
                self._eng.wait_ge(self._sem, st["issued"])
                st["waited"] = st["issued"]
            inst = fn(*a, **kw)
            inst.then_inc(self._sem, 1)
            st["issued"] += 1
            return inst
        return wrapped


class Sched:
    ENGS = ("pe", "dve", "act", "sp", "gp")
    SERIAL = {"dve", "act"}

    def __init__(self, nc, block, es):
        self.nc = nc
        self.block = block
        self.sem = {e: es.enter_context(nc.semaphore("sem_" + e)) for e in self.ENGS}
        self.sem["cc"] = es.enter_context(nc.semaphore("sem_cc"))
        self.cnt = {e: 0 for e in self.ENGS}
        self.cnt["cc"] = 0
        self.waited = {e: {e2: 0 for e2 in (*self.ENGS, "cc")} for e in self.ENGS}
        self.serial_state = {e: {"issued": 0, "waited": 0} for e in self.SERIAL}

    def step(self, eng, deps, body, n_dma=0):
        sem = self.sem
        waited = self.waited[eng]
        serial = eng in self.SERIAL
        use = []
        for d in deps:
            if d is None:
                continue
            de, dv = d
            if de == eng:
                if serial:
                    continue
            if dv > waited[de]:
                if de == eng and not n_dma and not serial:
                    continue
                use.append((de, dv))
                waited[de] = dv
        n_body_inst = [0]
        own_wait = 0
        if n_dma:
            own_wait = self.cnt[eng]
            if own_wait <= waited[eng]:
                own_wait = 0
            else:
                waited[eng] = own_wait
            self.cnt[eng] += 16 * n_dma
        issued = [0]

        def emit(e):
            if own_wait:
                e.wait_ge(sem[eng], own_wait)
            for de, dv in use:
                e.wait_ge(sem[de], dv)
            if n_dma:
                def dma(out, in_, **kw):
                    issued[0] += 1
                    return e.dma_start(out=out, in_=in_, **kw).then_inc(sem[eng], 16)
                body(e, dma)
                assert issued[0] == n_dma, f"declared {n_dma} dmas, issued {issued[0]}"
            elif serial:
                body(_SerialEng(e, sem[eng], self.serial_state[eng]))
            else:
                last = body(e)
                assert last is not None, "compute step must return its last instruction"
                last.then_inc(sem[eng], 1)

        if serial and not n_dma:
            getattr(self.block, ENG_ATTR[eng])(emit)
            self.cnt[eng] = self.serial_state[eng]["issued"]
            return (eng, self.cnt[eng])
        if not n_dma:
            self.cnt[eng] += 1
        getattr(self.block, ENG_ATTR[eng])(emit)
        return (eng, self.cnt[eng])

    def all_done(self):
        return [(e, self.cnt[e]) for e in self.ENGS]


def _finish(S, block):
    sem = S.sem
    cnts = {e: S.cnt[e] for e in S.ENGS}

    def gp_final(e):
        for e2 in S.ENGS:
            if cnts[e2]:
                e.wait_ge(sem[e2], cnts[e2])
    block.gpsimd(gp_final)


def mm_acc(pe, out, pairs):
    n = len(pairs)
    last = None
    for i, (l, r) in enumerate(pairs):
        last = pe.matmul(out, l, r, start=(i == 0), stop=(i == n - 1))
    return last


def hl3(lh, ll, rh, rl):
    return [(lh, rh), (lh, rl), (ll, rh)]


class _StopEmit(Exception):
    pass


def build_program(dbg=False, upto='p3b'):
    p2sub = upto.split('@', 1)[1] if upto.startswith('p2@') else None
    if p2sub:
        upto = 'p2'
    nc = bass.Bass("TRN2", target_bir_lowering=False)

    def dram_in(name, shape, dt=F32):
        return nc.dram_tensor(name, shape, dt, kind="ExternalInput").ap()

    def dram_scratch(name, shape, dt=F32):
        if dbg:
            return nc.dram_tensor(name, shape, dt, kind="ExternalOutput").ap()
        return nc.dram_tensor(name, shape, dt).ap()

    x_d = dram_in("x_seg", [TSEG, D])
    xT_d = dram_in("xT_seg", [D, TSEG])
    WT_d = {w: dram_in(w, [D, D]) for w in ("WqT", "WkT", "WvT", "WoT")}
    bq_d = dram_in("bq_t", [128, ND]); bk_d = dram_in("bk_t", [128, ND])
    bvb_d = dram_in("bv_b", [128, D]); bob_d = dram_in("bo_b", [128, D])
    bkb_d = dram_in("bk_b", [128, D])
    w1T_d = dram_in("w1T", [D, H]); w2T_d = dram_in("w2T", [H, D])
    triu_d = dram_in("triu", [128, 128])
    ident_d = dram_in("ident", [128, 128])
    pmask_d = dram_in("pmask", [128, 4])
    out_d = nc.dram_tensor("out", [TSEG, D], F32, kind="ExternalOutput").ap()

    spill_k = dram_scratch("spill_k", [NG, 2, 128, G * D], BF16)
    spill_v = dram_scratch("spill_v", [NG, 2, 128, G * D], BF16)
    spill_q = dram_scratch("spill_q", [NG, 2, 128, ND * GT], BF16)
    prefix_s = dram_scratch("prefix_s", [NG, 128, ND * D])
    prefix_z = dram_scratch("prefix_z", [NG, 128, ND])
    y1_dram = dram_scratch("y1_spill", [TSEG, D])
    hq_dram = dram_scratch("hq_sp", [NT, 128, H], BF16)
    w1q_dram = dram_scratch("w1q", [128, ND * H], BF16)
    w2q_dram = dram_scratch("w2q", [128, NH * D], BF16)
    sstart_d = dram_scratch("sstart", [128, ND * D])
    cc_in = nc.dram_tensor("cc_in", [D + 1, D], F32).ap()
    cc_out = nc.dram_tensor("cc_out", [4 * (D + 1), D], F32).ap()

    es = ExitStack()
    with es:
        block = es.enter_context(nc.Block())
        S = Sched(nc, block, es)
        sem = S.sem

        def SB(scope, name, shape, dt=F32):
            return scope.enter_context(nc.sbuf_tensor(name, shape, dt))

        def PS(scope, name, shape, dt=F32):
            return scope.enter_context(nc.psum_tensor(name, shape, dt))

        ones_bf = SB(es, "ones_bf", [128, 128], BF16)
        ident_f = SB(es, "ident_f", [128, 128], F32)
        ident_b = SB(es, "ident_b", [128, 128], BF16)
        triu_f = SB(es, "triu_f", [128, 128], F32)
        s1bc = SB(es, "s1bc", [128, 1]); s2bc = SB(es, "s2bc", [128, 1])
        thr1 = SB(es, "thr1", [128, 1]); nthr1 = SB(es, "nthr1", [128, 1])
        thr2 = SB(es, "thr2", [128, 1]); nthr2 = SB(es, "nthr2", [128, 1])
        bq_s = SB(es, "bq_s", [128, ND]); bk_s = SB(es, "bk_s", [128, ND])
        bvb_s = SB(es, "bvb_s", [128, D]); bob_s = SB(es, "bob_s", [128, D])
        bkb_s = SB(es, "bkb_s", [128, D])
        pmask_s = SB(es, "pmask_s", [128, 4])
        z_acc = SB(es, "z_acc", [128, ND]); z_start = SB(es, "z_start", [128, ND])
        zg = SB(es, "zg", [128, ND])
        zg_h = SB(es, "zg_h", [128, ND], BF16); zg_l = SB(es, "zg_l", [128, ND], BF16)
        f2all = SB(es, "f2all", [128, NT])
        eps_c = SB(es, "eps_c", [128, 1])
        s_scope = ExitStack()
        Smaster = SB(s_scope, "Smaster", [128, ND * D])

        def ld_const(e, dma):
            dma(out=ident_f[:], in_=ident_d[:])
            dma(out=triu_f[:], in_=triu_d[:])
            dma(out=bq_s[:], in_=bq_d[:])
            dma(out=bk_s[:], in_=bk_d[:])
            dma(out=bvb_s[:], in_=bvb_d[:])
            dma(out=bob_s[:], in_=bob_d[:])
            dma(out=bkb_s[:], in_=bkb_d[:])
            dma(out=pmask_s[:], in_=pmask_d[:])
        h_const = S.step("sp", [], ld_const, n_dma=8)

        def init_body(v):
            v.memset(ones_bf[:], 1.0)
            v.memset(eps_c[:], 1e-5)
            v.memset(z_acc[:], 0.0)
            v.memset(Smaster[:], 0.0)
            return v.tensor_copy(ident_b[:], ident_f[:])
        h_init = S.step("dve", [h_const], init_body)

        p0 = ExitStack()
        with p0:
            wpiece = SB(p0, "wpiece", [128, H])
            wpiece2 = SB(p0, "wpiece2", [128, H])
            tq2_ = SB(p0, "tq2_", [128, H], BF16)
            wsum = SB(p0, "wsum", [128, 1]); wtmp1 = SB(p0, "wtmp1", [128, 1])
            sh_ = SB(p0, "sh_", [128, 1], BF16); sl_ = SB(p0, "sl_", [128, 1], BF16)
            sc_row = SB(p0, "sc_row", [1, 2])
            sc_h = SB(p0, "sc_h", [1, 2], BF16); sc_l = SB(p0, "sc_l", [1, 2], BF16)
            cge = SB(p0, "cge", [128, H])
            tq_ = SB(p0, "tq_", [128, H], BF16)
            ps_s = PS(p0, "ps_s", [1, 2])
            ps_b = PS(p0, "ps_b", [128, 2])

            prev_wi_spill = None
            for wi, (wT_dd, npieces, pw, wq_dram, sbc, thr, nthr) in enumerate([
                    (w1T_d, ND, H, w1q_dram, s1bc, thr1, nthr1),
                    (w2T_d, NH, D, w2q_dram, s2bc, thr2, nthr2)]):
                hz = S.step("dve", [h_init], lambda v: v.memset(wsum[:], 0.0))
                hrs = [hz, hz]
                for pi in range(npieces):
                    buf = wpiece if pi % 2 == 0 else wpiece2

                    def ldp(e, dma, pi=pi, wT_dd=wT_dd, pw=pw, buf=buf):
                        dma(out=buf[:, :pw], in_=wT_dd[pi * 128:(pi + 1) * 128, :])
                    hl = S.step("sp", [hrs[pi % 2]], ldp, n_dma=1)

                    def red(v, pw=pw, buf=buf):
                        v.reduce_sum(wtmp1[:], buf[:, :pw], axis=AX.X,
                                     apply_absolute_value=True)
                        return v.tensor_add(wsum[:], wsum[:], wtmp1[:])
                    hrs[pi % 2] = S.step("dve", [hl], red)
                hr = hrs[(npieces - 1) % 2]

                def spl(v):
                    v.tensor_copy(sh_[:], wsum[:])
                    return v.tensor_sub(sl_[:], wsum[:], sh_[:])
                hsp = S.step("dve", [hr], spl)
                hmm = S.step("pe", [hsp], lambda p, wi=wi: mm_acc(
                    p, ps_s[0:1, wi:wi + 1],
                    [(sh_[:], ones_bf[:, 0:1]), (sl_[:], ones_bf[:, 0:1])]))
                nelem = float(npieces * 128 * pw)

                def mean(v, wi=wi, nelem=nelem):
                    v.tensor_scalar(sc_row[0:1, wi:wi + 1], ps_s[0:1, wi:wi + 1],
                                    1.0 / nelem, 1e-5, ALU.mult, ALU.max)
                    v.tensor_copy(sc_h[0:1, wi:wi + 1], sc_row[0:1, wi:wi + 1])
                    return v.tensor_sub(sc_l[0:1, wi:wi + 1], sc_row[0:1, wi:wi + 1],
                                        sc_h[0:1, wi:wi + 1])
                hmean = S.step("dve", [hmm], mean)
                hbc = S.step("pe", [hmean], lambda p, wi=wi: mm_acc(
                    p, ps_b[:, wi:wi + 1],
                    [(ones_bf[0:1, :], sc_h[0:1, wi:wi + 1]),
                     (ones_bf[0:1, :], sc_l[0:1, wi:wi + 1])]))

                def mkthr(v, sbc=sbc, thr=thr, nthr=nthr, wi=wi):
                    v.tensor_copy(sbc[:], ps_b[:, wi:wi + 1])
                    v.tensor_scalar(thr[:], sbc[:], 0.5, None, ALU.mult)
                    return v.tensor_scalar(nthr[:], sbc[:], -0.5, None, ALU.mult)
                hthr = S.step("dve", [hbc], mkthr)

                hcs = [hthr, hthr]
                hqs = [prev_wi_spill, prev_wi_spill]
                hq = hthr
                for pi in range(npieces):
                    buf = wpiece if pi % 2 == 0 else wpiece2
                    tqb = tq_ if pi % 2 == 0 else tq2_

                    def ldp2(e, dma, pi=pi, wT_dd=wT_dd, pw=pw, buf=buf):
                        dma(out=buf[:, :pw], in_=wT_dd[pi * 128:(pi + 1) * 128, :])
                    hl = S.step("sp", [hcs[pi % 2], prev_wi_spill if pi < 2 else None],
                                ldp2, n_dma=1)

                    def qz(v, pw=pw, thr=thr, nthr=nthr, buf=buf, tqb=tqb):
                        v.tensor_scalar(cge[:, :pw], buf[:, :pw], nthr[:, 0:1], None,
                                        ALU.is_le)
                        return v.scalar_tensor_tensor(
                            tqb[:, :pw], buf[:, :pw], thr[:, 0:1], cge[:, :pw],
                            ALU.is_ge, ALU.subtract)
                    hcmp = S.step("dve", [hl, hqs[pi % 2]], qz)
                    hcs[pi % 2] = hcmp

                    def stq(e, dma, pi=pi, wq_dram=wq_dram, pw=pw, tqb=tqb):
                        dma(out=wq_dram[:, pi * pw:(pi + 1) * pw], in_=tqb[:, :pw])
                    hq = S.step("gp", [hcmp], stq, n_dma=1)
                    hqs[pi % 2] = hq
                prev_wi_spill = hq
            h_p0_done = S.all_done()
        if upto == 'p0':
            _finish(S, block); s_scope.close(); return nc

        p1 = ExitStack()
        with p1:
            wk_h = SB(p1, "wk_h", [128, ND * D], BF16); wk_l = SB(p1, "wk_l", [128, ND * D], BF16)
            wv_h = SB(p1, "wv_h", [128, ND * D], BF16); wv_l = SB(p1, "wv_l", [128, ND * D], BF16)
            ws_scope = ExitStack()
            wstage = SB(ws_scope, "wstage", [128, ND * D])

            hws = h_p0_done
            for (wd, wh, wl) in [(WT_d["WkT"], wk_h, wk_l), (WT_d["WvT"], wv_h, wv_l)]:
                def ldw(e, dma, wd=wd):
                    dma(out=wstage[:].rearrange("p (dt o) -> p dt o", dt=ND),
                        in_=wd.rearrange("(dt p) o -> p dt o", p=128))
                hlw = S.step("sp", hws if isinstance(hws, list) else list(hws), ldw, n_dma=1)

                def splw_h(a, wh=wh):
                    return a.copy(wh[:], wstage[:])
                hwa = S.step("act", [hlw], splw_h)

                def splw_l(v, wh=wh, wl=wl):
                    return v.tensor_sub(wl[:], wstage[:], wh[:])
                hws = (hwa, S.step("dve", [hwa], splw_l))
            ws_scope.close()
            h_wkv = hws[1]

            xT_st = SB(p1, "xT_st", [128, ND * GT])
            xh = SB(p1, "xh", [128, ND * GT], BF16); xl = SB(p1, "xl", [128, ND * GT], BF16)
            kphi_e = SB(p1, "kphi_e", [128, D]); kphi_r = SB(p1, "kphi_r", [128, D])
            kpT_f = SB(p1, "kpT_f", [128, D])
            kT_h = SB(p1, "kT_h", [128, G * D], BF16); kT_l = SB(p1, "kT_l", [128, G * D], BF16)
            vT_h = SB(p1, "vT_h", [128, G * D], BF16); vT_l = SB(p1, "vT_l", [128, G * D], BF16)
            bkr_h = SB(p1, "bkr_h", [1, D], BF16); bkr_l = SB(p1, "bkr_l", [1, D], BF16)
            bvr_h = SB(p1, "bvr_h", [1, D], BF16); bvr_l = SB(p1, "bvr_l", [1, D], BF16)
            pk = PS(p1, "pk", [128, 2 * D])
            pv = PS(p1, "pv", [128, D])
            pkv = PS(p1, "pkv", [128, 2 * 512])

            def mk_rows_h(a):
                a.copy(bkr_h[:], bkb_s[0:1, :])
                return a.copy(bvr_h[:], bvb_s[0:1, :])
            hra = S.step("act", [h_const], mk_rows_h)

            def mk_rows_l(v):
                v.tensor_sub(bkr_l[:], bkb_s[0:1, :], bkr_h[:])
                return v.tensor_sub(bvr_l[:], bvb_s[0:1, :], bvr_h[:])
            hrl = S.step("dve", [hra], mk_rows_l)

            prev_split = h_wkv
            prev_x_pe = None
            prev_kv_done = h_init
            prev_spk = None; prev_spv = None
            prev_kv_pe = None
            for g in range(NG):
                def ldx(e, dma, g=g):
                    dma(out=xT_st[:].rearrange("p (dt t) -> p dt t", dt=ND),
                        in_=xT_d[:, g * GT:(g + 1) * GT]
                        .rearrange("(dt p) t -> p dt t", p=128))
                hx = S.step("sp", [prev_split], ldx, n_dma=1)

                hxa = S.step("act", [hx, prev_x_pe], lambda a: a.copy(xh[:], xT_st[:]))
                hxs = S.step("dve", [hxa], lambda v: v.tensor_sub(xl[:], xT_st[:], xh[:]))

                def spz(e, dma, g=g):
                    dma(out=prefix_z[g], in_=z_acc[:])
                    dma(out=prefix_s[g], in_=Smaster[:])
                hspill = S.step("gp", [prev_kv_done], spz, n_dma=2)

                hk_stt = None
                hk_hi = None
                hk_lo = None
                hz_mm = None
                for tt in range(G):
                    def mmk(p, tt=tt):
                        slot = tt % 2
                        last = None
                        for eh in range(2):
                            pairs = []
                            for dk in range(ND):
                                lh = xh[:, dk * GT + tt * 128: dk * GT + tt * 128 + 128]
                                ll = xl[:, dk * GT + tt * 128: dk * GT + tt * 128 + 128]
                                rh = wk_h[:, dk * D + eh * 512: dk * D + eh * 512 + 512]
                                rl = wk_l[:, dk * D + eh * 512: dk * D + eh * 512 + 512]
                                pairs += hl3(lh, ll, rh, rl)
                            pairs.append((ones_bf[0:1, :], bkr_h[0:1, eh * 512:(eh + 1) * 512]))
                            pairs.append((ones_bf[0:1, :], bkr_l[0:1, eh * 512:(eh + 1) * 512]))
                            last = mm_acc(p, pk[:, slot * D + eh * 512: slot * D + eh * 512 + 512],
                                          pairs)
                        return last
                    dep = [hxs, h_wkv, hrl]
                    if tt >= 2:
                        dep.append(hk_evac2[tt - 2])
                    hmk = S.step("pe", dep, mmk)

                    def ek_e(a, tt=tt):
                        return a.activation(kphi_e[:], pk[:, (tt % 2) * D:(tt % 2 + 1) * D],
                                            ACT.Exp)
                    dep = [hmk] + ([hk_stt] if hk_stt else [])
                    hea = S.step("act", dep, ek_e)

                    def ek_r(a, tt=tt):
                        return a.activation(kphi_r[:], pk[:, (tt % 2) * D:(tt % 2 + 1) * D],
                                            ACT.Relu)
                    her = S.step("act", [hea], ek_r)

                    def ek_stt(v, tt=tt):
                        return v.scalar_tensor_tensor(kpT_f[:], kphi_e[:], 1.0, kphi_r[:],
                                                      ALU.min, ALU.add)
                    hk_stt = S.step("dve", [her] + ([hk_hi] if hk_hi else []), ek_stt)

                    def ek_hi(a, tt=tt):
                        return a.copy(kT_h[:, tt * D:(tt + 1) * D], kpT_f[:])
                    hk_hi = S.step("act", [hk_stt, prev_spk, prev_kv_pe], ek_hi)

                    def ek_lo(v, tt=tt):
                        return v.tensor_sub(kT_l[:, tt * D:(tt + 1) * D], kpT_f[:],
                                            kT_h[:, tt * D:(tt + 1) * D])
                    hk_lo = S.step("dve", [hk_hi], ek_lo)
                    if tt == 0:
                        hk_evac2 = {}
                    hk_evac2[tt] = hk_lo

                    def zmm(p, tt=tt):
                        last = None
                        for dt in range(ND):
                            for j, pl in enumerate((kT_h, kT_l)):
                                st = (tt == 0 and dt == 0 and j == 0)
                                sp_ = (tt == G - 1 and dt == ND - 1 and j == 1)
                                last = p.matmul(pkv[:, dt:dt + 1],
                                                pl[:, tt * D + dt * 128: tt * D + dt * 128 + 128],
                                                ones_bf[:, 0:1], start=st, stop=sp_)
                        return last
                    dep = [hk_lo, hk_hi]
                    if tt == 0:
                        dep.append(prev_kv_done)
                    hz_mm = S.step("pe", dep, zmm)

                def evz(v):
                    return v.tensor_add(z_acc[:], z_acc[:], pkv[:, 0:ND])
                hz_ev = S.step("dve", [hz_mm, hspill], evz)

                def spk(e, dma, g=g):
                    dma(out=spill_k[g, 0], in_=kT_h[:])
                    dma(out=spill_k[g, 1], in_=kT_l[:])
                hspk = S.step("gp", [hk_lo, hk_hi], spk, n_dma=2)

                hv_hi = None
                hv_lo = None
                for tt in range(G):
                    def mmv(p, tt=tt):
                        last = None
                        for eh in range(2):
                            pairs = []
                            for dk in range(ND):
                                lh = xh[:, dk * GT + tt * 128: dk * GT + tt * 128 + 128]
                                ll = xl[:, dk * GT + tt * 128: dk * GT + tt * 128 + 128]
                                rh = wv_h[:, dk * D + eh * 512: dk * D + eh * 512 + 512]
                                rl = wv_l[:, dk * D + eh * 512: dk * D + eh * 512 + 512]
                                pairs += hl3(lh, ll, rh, rl)
                            pairs.append((ones_bf[0:1, :], bvr_h[0:1, eh * 512:(eh + 1) * 512]))
                            pairs.append((ones_bf[0:1, :], bvr_l[0:1, eh * 512:(eh + 1) * 512]))
                            last = mm_acc(p, pv[:, eh * 512:(eh + 1) * 512], pairs)
                        return last
                    dep = [hxs, h_wkv, hrl]
                    if tt >= 1:
                        dep.append(hv_lo)
                    hmv = S.step("pe", dep, mmv)

                    def ev_hi(a, tt=tt):
                        return a.copy(vT_h[:, tt * D:(tt + 1) * D], pv[:])
                    hv_hi = S.step("act", [hmv, prev_spv, prev_kv_pe], ev_hi)

                    def ev_lo(v, tt=tt):
                        return v.tensor_sub(vT_l[:, tt * D:(tt + 1) * D], pv[:],
                                            vT_h[:, tt * D:(tt + 1) * D])
                    hv_lo = S.step("dve", [hmv, hv_hi], ev_lo)

                def spv(e, dma, g=g):
                    dma(out=spill_v[g, 0], in_=vT_h[:])
                    dma(out=spill_v[g, 1], in_=vT_l[:])
                hspv = S.step("gp", [hv_lo, hv_hi], spv, n_dma=2)

                hkv_ev = []
                hm_last = None
                for dt in range(ND):
                    for eh in range(2):
                        idx = dt * 2 + eh

                        def mmkv(p, dt=dt, eh=eh, idx=idx):
                            pairs = []
                            for tc in range(G):
                                lh = kT_h[:, tc * D + dt * 128: tc * D + dt * 128 + 128]
                                ll = kT_l[:, tc * D + dt * 128: tc * D + dt * 128 + 128]
                                rh = vT_h[:, tc * D + eh * 512: tc * D + eh * 512 + 512]
                                rl = vT_l[:, tc * D + eh * 512: tc * D + eh * 512 + 512]
                                pairs += hl3(lh, ll, rh, rl)
                            slot = idx % 2
                            return mm_acc(p, pkv[:, slot * 512:(slot + 1) * 512], pairs)
                        dep = [hk_lo, hk_hi, hv_lo, hv_hi, hz_ev]
                        if idx >= 2:
                            dep.append(hkv_ev[idx - 2])
                        hm_last = S.step("pe", dep, mmkv)

                        def evkv(v, dt=dt, eh=eh, idx=idx):
                            slot = idx % 2
                            dst = Smaster[:, dt * D + eh * 512: dt * D + eh * 512 + 512]
                            return v.tensor_add(dst, dst, pkv[:, slot * 512:(slot + 1) * 512])
                        hkv_ev.append(S.step("dve", [hm_last, hspill], evkv))

                prev_kv_done = hkv_ev[-1]
                prev_kv_pe = hm_last
                prev_x_pe = hm_last
                prev_split = hxs
                prev_spk = hspk; prev_spv = hspv

            h_p1a_done = S.all_done()

        if upto == 'p1a':
            _finish(S, block); s_scope.close(); return nc

        def wr_cc(e, dma):
            for dt in range(ND):
                dma(out=cc_in[dt * 128:(dt + 1) * 128, :],
                    in_=Smaster[:, dt * D:(dt + 1) * D])
            for dt in range(ND):
                dma(out=cc_in[D:D + 1, dt * 128:(dt + 1) * 128],
                    in_=z_acc[:, dt:dt + 1])
        h_wrcc = S.step("gp", h_p1a_done, wr_cc, n_dma=16)

        _wr_val = h_wrcc[1]

        def _cc_emit(e):
            e.wait_ge(S.sem["gp"], _wr_val)
            e.collective_compute(
                "AllGather", ALU.bypass, replica_groups=REPLICA_GROUPS,
                ins=[cc_in[:]], outs=[cc_out[:]]).then_inc(S.sem["cc"], 1)
        block.gpsimd(_cc_emit)
        S.cnt["cc"] = 1
        h_cc = ("cc", 1)

        p1b = ExitStack()
        with p1b:
            wq_h = SB(p1b, "wq_h", [128, ND * D], BF16); wq_l = SB(p1b, "wq_l", [128, ND * D], BF16)
            ws2_scope = ExitStack()
            wstage2 = SB(ws2_scope, "wstage2", [128, ND * D])

            def ldwq(e, dma):
                dma(out=wstage2[:].rearrange("p (dt o) -> p dt o", dt=ND),
                    in_=WT_d["WqT"].rearrange("(dt p) o -> p dt o", p=128))
            hlq = S.step("sp", h_p1a_done, ldwq, n_dma=1)

            def splwq(v):
                v.tensor_copy(wq_h[:], wstage2[:])
                return v.tensor_sub(wq_l[:], wstage2[:], wq_h[:])
            hwq = S.step("dve", [hlq], splwq)
            ws2_scope.close()

            xT_st2 = SB(p1b, "xT_st2", [128, ND * GT])
            xh2 = SB(p1b, "xh2", [128, ND * GT], BF16); xl2 = SB(p1b, "xl2", [128, ND * GT], BF16)
            qD_f = SB(p1b, "qD_f", [128, ND * GT])
            q_h = SB(p1b, "q_h", [128, ND * GT], BF16); q_l = SB(p1b, "q_l", [128, ND * GT], BF16)
            ephi2 = SB(p1b, "ephi2", [128, GT]); rphi2 = SB(p1b, "rphi2", [128, GT])
            pq = PS(p1b, "pq", [128, 2 * GT])

            prev_split = hwq
            prev_x_pe = None
            prev_spq = None
            for g in range(NG):
                def ldx2(e, dma, g=g):
                    dma(out=xT_st2[:].rearrange("p (dt t) -> p dt t", dt=ND),
                        in_=xT_d[:, g * GT:(g + 1) * GT]
                        .rearrange("(dt p) t -> p dt t", p=128))
                hx = S.step("sp", [prev_split], ldx2, n_dma=1)

                def splx2(v):
                    v.tensor_copy(xh2[:], xT_st2[:])
                    return v.tensor_sub(xl2[:], xT_st2[:], xh2[:])
                hxs = S.step("dve", [hx, hwq, prev_x_pe], splx2)

                hq_evac = []
                hmq = None
                for ot in range(ND):
                    def mmq(p, ot=ot):
                        pairs = []
                        for dk in range(ND):
                            lh = wq_h[:, dk * D + ot * 128: dk * D + ot * 128 + 128]
                            ll = wq_l[:, dk * D + ot * 128: dk * D + ot * 128 + 128]
                            rh = xh2[:, dk * GT:(dk + 1) * GT]
                            rl = xl2[:, dk * GT:(dk + 1) * GT]
                            pairs += hl3(lh, ll, rh, rl)
                        return mm_acc(p, pq[:, (ot % 2) * GT:(ot % 2 + 1) * GT], pairs)
                    dep = [hxs] + ([hq_evac[ot - 2][1]] if ot >= 2 else [])
                    hmq = S.step("pe", dep, mmq)

                    def eq_act(a, ot=ot):
                        a.activation(ephi2[:], pq[:, (ot % 2) * GT:(ot % 2 + 1) * GT],
                                     ACT.Exp, bias=bq_s[:, ot:ot + 1])
                        return a.activation(rphi2[:], pq[:, (ot % 2) * GT:(ot % 2 + 1) * GT],
                                            ACT.Relu, bias=bq_s[:, ot:ot + 1])
                    heq = S.step("act", [hmq] + ([hq_evac[ot - 1][1]] if ot >= 1 else []),
                                 eq_act)

                    def eq_stt(v, ot=ot):
                        return v.scalar_tensor_tensor(qD_f[:, ot * GT:(ot + 1) * GT],
                                                      ephi2[:], 1.0, rphi2[:],
                                                      ALU.min, ALU.add)
                    hst = S.step("dve", [heq, prev_spq, prev_x_pe], eq_stt)

                    def eq_hi(a, ot=ot):
                        return a.copy(q_h[:, ot * GT:(ot + 1) * GT],
                                      qD_f[:, ot * GT:(ot + 1) * GT])
                    hhi = S.step("act", [hst], eq_hi)

                    def eq_lo(v, ot=ot):
                        return v.tensor_sub(q_l[:, ot * GT:(ot + 1) * GT],
                                            qD_f[:, ot * GT:(ot + 1) * GT],
                                            q_h[:, ot * GT:(ot + 1) * GT])
                    hev = S.step("dve", [hhi], eq_lo)
                    hq_evac.append((hmq, hev))

                def spq(e, dma, g=g):
                    dma(out=spill_q[g, 0], in_=q_h[:])
                    dma(out=spill_q[g, 1], in_=q_l[:])
                prev_spq = S.step("gp", [hq_evac[-1][1]], spq, n_dma=2)
                prev_split = hxs
                prev_x_pe = hmq

            h_p1b_done = S.all_done()

        if upto == 'p1b':
            _finish(S, block); s_scope.close(); return nc

        px = ExitStack()
        with px:
            Ttmp = SB(px, "Ttmp", [128, ND * D])
            ztmp8 = SB(px, "ztmp8", [128, ND])
            hacc = None
            for r in range(4):
                def ldT(e, dma, r=r):
                    base = r * (D + 1)
                    dma(out=Ttmp[:].rearrange("p (dt e) -> p dt e", dt=ND),
                        in_=cc_out[base:base + D, :].rearrange("(dt p) e -> p dt e", p=128))
                    dma(out=ztmp8[:],
                        in_=cc_out[base + D:base + D + 1, :]
                        .rearrange("r (dt p) -> p (r dt)", p=128, r=1),
                        allow_slow_non_contiguous=True)
                dep = [h_cc] + ([hacc] if hacc else []) + (h_p1b_done if r == 0 else [])
                hl = S.step("sp", dep, ldT, n_dma=2)

                def accT(v, r=r):
                    if r == 0:
                        v.tensor_scalar(Smaster[:], Ttmp[:], pmask_s[:, 0:1], None, ALU.mult)
                        return v.tensor_scalar(z_start[:], ztmp8[:], pmask_s[:, 0:1],
                                               None, ALU.mult)
                    v.scalar_tensor_tensor(Smaster[:], Ttmp[:], pmask_s[:, r:r + 1],
                                           Smaster[:], ALU.mult, ALU.add)
                    return v.scalar_tensor_tensor(z_start[:], ztmp8[:], pmask_s[:, r:r + 1],
                                                  z_start[:], ALU.mult, ALU.add)
                hacc = S.step("dve", [hl] + h_p1b_done, accT)

            def spss(e, dma):
                dma(out=sstart_d[:], in_=Smaster[:])
            h_spss = S.step("gp", [hacc], spss, n_dma=1)
            h_x2_done = S.all_done()
        s_scope.close()
        if upto == 'x2':
            _finish(S, block); return nc

        p2 = ExitStack()
        with p2:
            wo_h = SB(p2, "wo_h", [128, ND * D], BF16); wo_l = SB(p2, "wo_l", [128, ND * D], BF16)
            ws3_scope = ExitStack()
            wstage3 = SB(ws3_scope, "wstage3", [128, ND * D])

            def ldwo(e, dma):
                dma(out=wstage3[:].rearrange("p (dt o) -> p dt o", dt=ND),
                    in_=WT_d["WoT"].rearrange("(dt p) o -> p dt o", p=128))
            hlo = S.step("sp", h_x2_done, ldwo, n_dma=1)

            def splwo(v):
                v.tensor_copy(wo_h[:], wstage3[:])
                return v.tensor_sub(wo_l[:], wstage3[:], wo_h[:])
            hwo = S.step("dve", [hlo], splwo)
            ws3_scope.close()
            if p2sub == 'w':
                _finish(S, block)
                return nc

            Sgrp = SB(p2, "Sgrp", [128, ND * D])
            s_h = SB(p2, "s_h", [128, ND * D], BF16); s_l = SB(p2, "s_l", [128, ND * D], BF16)
            q2_h = SB(p2, "q2_h", [128, ND * GT], BF16); q2_l = SB(p2, "q2_l", [128, ND * GT], BF16)
            k2_h = SB(p2, "k2_h", [128, G * D], BF16); k2_l = SB(p2, "k2_l", [128, G * D], BF16)
            v2_h = SB(p2, "v2_h", [128, G * D], BF16); v2_l = SB(p2, "v2_l", [128, G * D], BF16)
            kd_h = SB(p2, "kd_h", [128, ND * 128], BF16); kd_l = SB(p2, "kd_l", [128, ND * 128], BF16)
            aT_h = SB(p2, "aT_h", [128, G * GT], BF16); aT_l = SB(p2, "aT_l", [128, G * GT], BF16)
            n_h = SB(p2, "n_h", [128, ND * GT], BF16); n_l = SB(p2, "n_l", [128, ND * GT], BF16)
            atmp = SB(p2, "atmp", [128, 128])
            dcol = SB(p2, "dcol", [128, G])
            recip_den = SB(p2, "recip_den", [128, G])
            x_res = SB(p2, "x_res", [128, D])
            y_t = SB(p2, "y_t", [128, D])
            y1_t = SB(p2, "y1_t", [128, D])
            zgtmp = SB(p2, "zgtmp", [128, ND])
            bnb = SB(p2, "bnb", [128, 12]); bna = SB(p2, "bna", [128, 2])
            sd_c = SB(p2, "sd_c", [128, 1]); rstd_c = SB(p2, "rstd_c", [128, 1])
            negm_c = SB(p2, "negm_c", [128, 1])
            pA = PS(p2, "pA", [128, 512])
            pden = PS(p2, "pden", [128, G])
            pnum2 = PS(p2, "pnum2", [128, 2 * 512])
            pwo = PS(p2, "pwo", [128, D])
            ptrbA = PS(p2, "ptrbA", [128, 512], BF16)
            ptrbB = PS(p2, "ptrbB", [128, 512], BF16)

            h_prev_group_end = None
            prev_hey = None
            prev_hsd = None
            prev_sty1 = None
            prev_hxr_reader = None
            prev_trev = None
            prev_mn_pe = None
            h_ldsz = None
            h_accS = None
            for g in range(NG if not p2sub else 1):
              try:
                  if h_ldsz is None:
                      def ldsz0(e, dma, g=g):
                          dma(out=Sgrp[:], in_=prefix_s[g])
                          dma(out=zgtmp[:], in_=prefix_z[g])
                      h_ldsz = S.step("sp", h_x2_done + [hwo], ldsz0, n_dma=2)

                      def accS0(e, dma):
                          for i in range(4):
                              dma(out=Sgrp[:, i * 2048:(i + 1) * 2048],
                                  in_=sstart_d[:, i * 2048:(i + 1) * 2048],
                                  accum_op=ALU.add)
                      h_accS = S.step("gp", [h_ldsz], accS0, n_dma=4)

                  def ldkqv(e, dma, g=g):
                      dma(out=q2_h[:], in_=spill_q[g, 0]); dma(out=q2_l[:], in_=spill_q[g, 1])
                      dma(out=k2_h[:], in_=spill_k[g, 0]); dma(out=k2_l[:], in_=spill_k[g, 1])
                      dma(out=v2_h[:], in_=spill_v[g, 0]); dma(out=v2_l[:], in_=spill_v[g, 1])
                  hld = S.step("sp", [prev_mn_pe], ldkqv, n_dma=6)
                  if p2sub == 'ld0':
                      raise _StopEmit

                  hz0 = S.step("dve", [h_ldsz, prev_mn_pe], lambda v: v.tensor_add(
                      zg[:], zgtmp[:], z_start[:]))

                  def mks_hi(a):
                      a.copy(s_h[:], Sgrp[:])
                      return a.copy(zg_h[:], zg[:])
                  hSa = S.step("act", [h_ldsz, h_accS, hz0, prev_mn_pe], mks_hi)

                  def mks_lo(v):
                      v.tensor_sub(s_l[:], Sgrp[:], s_h[:])
                      return v.tensor_sub(zg_l[:], zg[:], zg_h[:])
                  hS = S.step("dve", [hSa], mks_lo)
                  if p2sub == 'ld':
                      raise _StopEmit

                  if g + 1 < NG:
                      def ldsz(e, dma, g=g):
                          dma(out=Sgrp[:], in_=prefix_s[g + 1])
                          dma(out=zgtmp[:], in_=prefix_z[g + 1])
                      h_ldsz = S.step("sp", [hS], ldsz, n_dma=2)

                      def accSn(e, dma):
                          for i in range(4):
                              dma(out=Sgrp[:, i * 2048:(i + 1) * 2048],
                                  in_=sstart_d[:, i * 2048:(i + 1) * 2048],
                                  accum_op=ALU.add)
                      h_accS = S.step("gp", [h_ldsz], accSn, n_dma=4)

                  hA_ev = []
                  for st_ in range(G):
                      htr_ev = None
                      for (ksrc, kdst) in ((k2_h, kd_h), (k2_l, kd_l)):
                          for half in range(2):
                              ptrb = ptrbA if half == 0 else ptrbB

                              def trk2(p, st_=st_, half=half, ksrc=ksrc, ptrb=ptrb):
                                  last = None
                                  for j in range(4):
                                      dk = half * 4 + j
                                      last = p.transpose(
                                          ptrb[:, j * 128:(j + 1) * 128],
                                          ksrc[:, st_ * D + dk * 128: st_ * D + dk * 128 + 128],
                                          ident_b[:])
                                  return last
                              dep = [hld, prev_trev]
                              if hA_ev:
                                  dep.append(hA_ev[-1])
                              ht = S.step("pe", dep, trk2)

                              def evt(a, half=half, kdst=kdst, ptrb=ptrb):
                                  return a.copy(kdst[:, half * 512:(half + 1) * 512],
                                                ptrb[:])
                              prev_trev = S.step("act", [ht], evt)
                          htr_ev = prev_trev

                      ncols = GT - st_ * 128

                      def mma(p, st_=st_, ncols=ncols):
                          pairs = []
                          for dk in range(ND):
                              lh = kd_h[:, dk * 128:(dk + 1) * 128]
                              ll = kd_l[:, dk * 128:(dk + 1) * 128]
                              rh = q2_h[:, dk * GT + st_ * 128: dk * GT + GT]
                              rl = q2_l[:, dk * GT + st_ * 128: dk * GT + GT]
                              pairs += hl3(lh, ll, rh, rl)
                          return mm_acc(p, pA[:, 0:ncols], pairs)
                      dep = [htr_ev, hS]
                      if hA_ev:
                          dep.append(hA_ev[-1])
                      hma = S.step("pe", dep, mma)

                      def eva(v, st_=st_, ncols=ncols):
                          base = st_ * GT + st_ * 128
                          v.tensor_mul(atmp[:], pA[:, 0:128], triu_f[:])
                          v.tensor_copy(aT_h[:, base:base + 128], atmp[:])
                          v.scalar_tensor_tensor(aT_l[:, base:base + 128], atmp[:],
                                                 1.0, aT_h[:, base:base + 128],
                                                 ALU.mult, ALU.subtract)
                          if ncols > 128:
                              v.tensor_copy(aT_h[:, base + 128: st_ * GT + GT],
                                            pA[:, 128:ncols])
                              return v.scalar_tensor_tensor(
                                  aT_l[:, base + 128: st_ * GT + GT], pA[:, 128:ncols],
                                  1.0, aT_h[:, base + 128: st_ * GT + GT],
                                  ALU.mult, ALU.subtract)
                          return v.engine_nop()
                      hA_ev.append(S.step("dve", [hma], eva))
                  if p2sub == 'A':
                      raise _StopEmit

                  def mden(p):
                      last = None
                      first = True
                      for cq in range(G):
                          out = pden[:, cq:cq + 1]
                          for st_ in range(cq + 1):
                              base = st_ * GT + cq * 128
                              for pl in (aT_h, aT_l):
                                  last = p.matmul(out, pl[:, base:base + 128],
                                                  ones_bf[:, 0:1],
                                                  start=first, stop=False)
                                  first = False
                          for dk in range(ND):
                              qh_sl = q2_h[:, dk * GT + cq * 128: dk * GT + cq * 128 + 128]
                              ql_sl = q2_l[:, dk * GT + cq * 128: dk * GT + cq * 128 + 128]
                              for (qp, zp) in ((qh_sl, zg_h), (ql_sl, zg_h), (qh_sl, zg_l)):
                                  isl = (cq == G - 1 and dk == ND - 1 and
                                         qp is qh_sl and zp is zg_l)
                                  last = p.matmul(out, qp, zp[:, dk:dk + 1],
                                                  start=False, stop=isl)
                      return last
                  hmden = S.step("pe", hA_ev + [hS], mden)

                  def evden(v):
                      v.tensor_scalar(dcol[:], pden[:], 1e-6, None, ALU.add)
                      return v.reciprocal(recip_den[:], dcol[:])
                  hevden = S.step("dve", [hmden], evden)
                  if p2sub == 'den':
                      raise _StopEmit

                  h_ev_lo = {}
                  h_ev_hi = {}
                  hmn = None
                  for et in range(ND):
                      def mnum(p, et=et):
                          slot = et % 2
                          pairs = []
                          for dk in range(ND):
                              sh_sl = s_h[:, dk * D + et * 128: dk * D + et * 128 + 128]
                              sl_sl = s_l[:, dk * D + et * 128: dk * D + et * 128 + 128]
                              qh_sl = q2_h[:, dk * GT:(dk + 1) * GT]
                              ql_sl = q2_l[:, dk * GT:(dk + 1) * GT]
                              pairs += hl3(sh_sl, sl_sl, qh_sl, ql_sl)
                          n = len(pairs)
                          last = None
                          for i, (l, r) in enumerate(pairs):
                              last = p.matmul(pnum2[:, slot * 512:(slot + 1) * 512],
                                              l, r, start=(i == 0), stop=False)
                          for st_ in range(G):
                              vh_sl = v2_h[:, st_ * D + et * 128: st_ * D + et * 128 + 128]
                              vl_sl = v2_l[:, st_ * D + et * 128: st_ * D + et * 128 + 128]
                              ah_sl = aT_h[:, st_ * GT + st_ * 128: st_ * GT + GT]
                              al_sl = aT_l[:, st_ * GT + st_ * 128: st_ * GT + GT]
                              ipairs = [(vh_sl, ah_sl), (vh_sl, al_sl), (vl_sl, ah_sl)]
                              for j, (l, r) in enumerate(ipairs):
                                  isl = (st_ == G - 1 and j == 2)
                                  last = p.matmul(
                                      pnum2[:, slot * 512 + st_ * 128:(slot + 1) * 512],
                                      l, r, start=False, stop=isl)
                          return last
                      dep = [hS, hA_ev[-1]]
                      if et >= 2:
                          dep += [h_ev_lo[et - 2], h_ev_hi[et - 2]]
                      hmn = S.step("pe", dep, mnum)

                      def ev_hi(a, et=et):
                          return a.copy(n_h[:, et * GT:(et + 1) * GT],
                                        pnum2[:, (et % 2) * 512:(et % 2 + 1) * 512])
                      h_ev_hi[et] = S.step("act", [hmn], ev_hi)

                      def ev_lo(v, et=et):
                          return v.scalar_tensor_tensor(
                              n_l[:, et * GT:(et + 1) * GT],
                              pnum2[:, (et % 2) * 512:(et % 2 + 1) * 512],
                              1.0, n_h[:, et * GT:(et + 1) * GT],
                              ALU.mult, ALU.subtract)
                      h_ev_lo[et] = S.step("dve", [hmn, h_ev_hi[et]], ev_lo)
                  prev_mn_pe = hmn
                  if p2sub == 'num':
                      raise _StopEmit

                  for cq in range(G):
                      def mwo(p, cq=cq):
                          last = None
                          for eh in range(2):
                              pairs = []
                              for ek in range(ND):
                                  lh = n_h[:, ek * GT + cq * 128: ek * GT + cq * 128 + 128]
                                  ll = n_l[:, ek * GT + cq * 128: ek * GT + cq * 128 + 128]
                                  rh = wo_h[:, ek * D + eh * 512: ek * D + eh * 512 + 512]
                                  rl = wo_l[:, ek * D + eh * 512: ek * D + eh * 512 + 512]
                                  pairs += hl3(lh, ll, rh, rl)
                              last = mm_acc(p, pwo[:, eh * 512:(eh + 1) * 512], pairs)
                          return last
                      hmwo = S.step("pe", [h_ev_lo[ND - 1], h_ev_hi[ND - 1], prev_hey], mwo)
                      if p2sub == 'wo':
                          raise _StopEmit

                      def ldxr(e, dma, g=g, cq=cq):
                          t0 = (g * G + cq) * 128
                          dma(out=x_res[:], in_=x_d[t0:t0 + 128, :])
                      hxr = S.step("sp", [prev_hxr_reader], ldxr, n_dma=1)

                      def evy(v, cq=cq):
                          v.scalar_tensor_tensor(y_t[:], pwo[:], recip_den[:, cq:cq + 1],
                                                 bob_s[:], ALU.mult, ALU.add)
                          v.tensor_add(y_t[:], y_t[:], x_res[:])
                          v.bn_stats(bnb[:, 0:6], y_t[:, 0:512])
                          v.bn_stats(bnb[:, 6:12], y_t[:, 512:1024])
                          v.bn_aggr(bna[:, 0:2], bnb[:, 0:12])
                          return v.tensor_scalar(negm_c[:], bna[:, 0:1], -1.0, None, ALU.mult)
                      hey = S.step("dve", [hmwo, hxr, hevden, prev_hsd], evy)
                      prev_hey = hey
                      prev_hxr_reader = hey

                      hsd = S.step("act", [hey], lambda a: a.activation(
                          sd_c[:], bna[:, 1:2], ACT.Sqrt, bias=eps_c[:, 0:1]))
                      prev_hsd = hsd

                      def evy2(v):
                          v.reciprocal(rstd_c[:], sd_c[:])
                          return v.tensor_scalar(y1_t[:], y_t[:], negm_c[:, 0:1],
                                                 rstd_c[:, 0:1], ALU.add, ALU.mult)
                      hy1 = S.step("dve", [hsd, prev_sty1], evy2)

                      def sty1(e, dma, g=g, cq=cq):
                          t0 = (g * G + cq) * 128
                          dma(out=y1_dram[t0:t0 + 128, :], in_=y1_t[:])
                      prev_sty1 = S.step("gp", [hy1], sty1, n_dma=1)

                  h_prev_group_end = hy1
              except _StopEmit:
                pass

            h_p2_done = S.all_done()
            if p2sub:
                pass

        if upto == 'p2':
            _finish(S, block); return nc

        p3a = ExitStack()
        with p3a:
            w1q_s = SB(p3a, "w1q_s", [128, ND * H], BF16)
            y1_ld = SB(p3a, "y1_ld", [128, D])
            m1c = SB(p3a, "m1c", [128, 1]); r1c = SB(p3a, "r1c", [128, 1])
            f1c = SB(p3a, "f1c", [128, 1])
            m2c = SB(p3a, "m2c", [128, 1]); r2c = SB(p3a, "r2c", [128, 1])
            mtmp = SB(p3a, "mtmp", [128, 1])
            xi_ = SB(p3a, "xi_", [128, D], I32)
            xb_ = SB(p3a, "xb_", [128, D], BF16)
            yqD = SB(p3a, "yqD", [128, ND * 128], BF16)
            h_f = SB(p3a, "h_f", [128, H])
            hi_ = SB(p3a, "hi_", [128, H], I32)
            hb_ = SB(p3a, "hb_", [128, H], BF16)
            ph_ = PS(p3a, "ph_", [128, 2 * 512])
            ptr3 = PS(p3a, "ptr3", [128, 128], BF16)

            hw1 = S.step("sp", h_p2_done, lambda e, dma: dma(out=w1q_s[:], in_=w1q_dram[:])
                         and None, n_dma=1)

            prev_hq2 = None
            prev_trq_pe = None
            prev_gel_last = None
            prev_sth = None
            prev_hly_gate = hw1
            for ti in range(NT):
                def ldy(e, dma, ti=ti):
                    dma(out=y1_ld[:], in_=y1_dram[ti * 128:(ti + 1) * 128, :])
                hly = S.step("sp", [prev_hly_gate], ldy, n_dma=1)

                def q1(v):
                    v.reduce_max(m1c[:], y1_ld[:], axis=AX.X, apply_absolute_value=True)
                    v.tensor_scalar(m1c[:], m1c[:], 1e-5, None, ALU.max)
                    v.reciprocal(r1c[:], m1c[:])
                    v.tensor_scalar(f1c[:], m1c[:], s1bc[:, 0:1], 1.0 / 127.0,
                                    ALU.mult, ALU.mult)
                    v.tensor_scalar(xi_[:], y1_ld[:], r1c[:, 0:1], 127.0,
                                    ALU.mult, ALU.mult)
                    return v.tensor_copy(xb_[:], xi_[:])
                hq1 = S.step("dve", [hly, prev_trq_pe, prev_gel_last], q1)

                htr_ev = None
                htr_pe = None
                for dt in range(ND):
                    def trq(p, dt=dt):
                        return p.transpose(ptr3[:], xb_[:, dt * 128:(dt + 1) * 128],
                                           ident_b[:])
                    dep = [hq1] + ([htr_ev] if htr_ev else [])
                    htr_pe = S.step("pe", dep, trq)

                    def evq(v, dt=dt):
                        return v.tensor_copy(yqD[:, dt * 128:(dt + 1) * 128], ptr3[:])
                    htr_ev = S.step("dve", [htr_pe], evq)
                prev_trq_pe = htr_pe

                hml = []
                for hh in range(8):
                    def mm1(p, hh=hh):
                        pairs = [(yqD[:, dk * 128:(dk + 1) * 128],
                                  w1q_s[:, dk * H + hh * 512: dk * H + hh * 512 + 512])
                                 for dk in range(ND)]
                        return mm_acc(p, ph_[:, (hh % 2) * 512:(hh % 2 + 1) * 512], pairs)
                    dep = [htr_ev, hw1] + ([hml[-2][1]] if len(hml) >= 2 else [])
                    hm = S.step("pe", dep, mm1)

                    def gel(a, hh=hh):
                        return a.activation(h_f[:, hh * 512:(hh + 1) * 512],
                                            ph_[:, (hh % 2) * 512:(hh % 2 + 1) * 512],
                                            ACT.Gelu, scale=f1c[:, 0:1])
                    gdep = [hm] + ([prev_hq2] if (hh == 0 and prev_hq2) else [])
                    hg = S.step("act", gdep, gel)

                    def mx(v, hh=hh):
                        if hh == 0:
                            return v.reduce_max(m2c[:], h_f[:, 0:512], axis=AX.X,
                                                apply_absolute_value=True)
                        v.reduce_max(mtmp[:], h_f[:, hh * 512:(hh + 1) * 512], axis=AX.X,
                                     apply_absolute_value=True)
                        return v.tensor_max(m2c[:], m2c[:], mtmp[:])
                    hx_ = S.step("dve", [hg], mx)
                    hml.append((hm, hg, hx_))
                prev_gel_last = hml[-1][1]

                def q2(v, ti=ti):
                    v.tensor_scalar(m2c[:], m2c[:], 1e-5, None, ALU.max)
                    v.reciprocal(r2c[:], m2c[:])
                    v.tensor_scalar(f2all[:, ti:ti + 1], m2c[:], s2bc[:, 0:1], 1.0 / 127.0,
                                    ALU.mult, ALU.mult)
                    v.tensor_scalar(hi_[:], h_f[:], r2c[:, 0:1], 127.0, ALU.mult, ALU.mult)
                    return v.tensor_copy(hb_[:], hi_[:])
                hq2 = S.step("dve", [hml[-1][2], prev_sth], q2)
                prev_hq2 = hq2

                def sth(e, dma, ti=ti):
                    dma(out=hq_dram[ti], in_=hb_[:])
                prev_sth = S.step("gp", [hq2], sth, n_dma=1)
                prev_hly_gate = hq1

            h_p3a_done = S.all_done()

        if upto == 'p3a':
            _finish(S, block); return nc

        p3b = ExitStack()
        with p3b:
            w2q_s = SB(p3b, "w2q_s", [128, NH * D], BF16)
            hb2 = SB(p3b, "hb2", [128, H], BF16)
            y1_ld2 = SB(p3b, "y1_ld2", [128, D])
            hqD = SB(p3b, "hqD", [128, NH * 128], BF16)
            y2_t = SB(p3b, "y2_t", [128, D])
            out_t = SB(p3b, "out_t", [128, D])
            bnb2 = SB(p3b, "bnb2", [128, 12]); bna2 = SB(p3b, "bna2", [128, 2])
            sd2 = SB(p3b, "sd2", [128, 1]); rstd2 = SB(p3b, "rstd2", [128, 1])
            negm2 = SB(p3b, "negm2", [128, 1])
            po_ = PS(p3b, "po_", [128, D])
            ptr4 = PS(p3b, "ptr4", [128, 128], BF16)

            hw2 = S.step("sp", h_p3a_done, lambda e, dma: dma(out=w2q_s[:], in_=w2q_dram[:])
                         and None, n_dma=1)

            prev_hf = None
            prev_hm2 = None
            prev_hs2 = None
            prev_stout = None
            h_last_out = None
            for ti in range(NT):
                def ldh(e, dma, ti=ti):
                    dma(out=hb2[:], in_=hq_dram[ti])
                    dma(out=y1_ld2[:], in_=y1_dram[ti * 128:(ti + 1) * 128, :])
                dep = [hw2] if ti == 0 else [prev_hf, prev_hm2]
                hlh = S.step("sp", dep, ldh, n_dma=2)

                htr_ev = None
                for hk in range(NH):
                    def trh(p, hk=hk):
                        return p.transpose(ptr4[:], hb2[:, hk * 128:(hk + 1) * 128],
                                           ident_b[:])
                    dep = [hlh] + ([htr_ev] if htr_ev else [])
                    ht = S.step("pe", dep, trh)

                    def evh(v, hk=hk):
                        return v.tensor_copy(hqD[:, hk * 128:(hk + 1) * 128], ptr4[:])
                    htr_ev = S.step("dve", [ht], evh)

                def mm2(p):
                    last = None
                    for eh in range(2):
                        pairs = [(hqD[:, hk * 128:(hk + 1) * 128],
                                  w2q_s[:, hk * D + eh * 512: hk * D + eh * 512 + 512])
                                 for hk in range(NH)]
                        last = mm_acc(p, po_[:, eh * 512:(eh + 1) * 512], pairs)
                    return last
                hm2 = S.step("pe", [htr_ev, hw2], mm2)
                prev_hm2 = hm2

                def evy2b(v, ti=ti):
                    v.scalar_tensor_tensor(y2_t[:], po_[:], f2all[:, ti:ti + 1],
                                           y1_ld2[:], ALU.mult, ALU.add)
                    v.bn_stats(bnb2[:, 0:6], y2_t[:, 0:512])
                    v.bn_stats(bnb2[:, 6:12], y2_t[:, 512:1024])
                    v.bn_aggr(bna2[:, 0:2], bnb2[:, 0:12])
                    return v.tensor_scalar(negm2[:], bna2[:, 0:1], -1.0, None, ALU.mult)
                he2 = S.step("dve", [hm2, hlh, prev_hs2], evy2b)

                hs2 = S.step("act", [he2], lambda a: a.activation(
                    sd2[:], bna2[:, 1:2], ACT.Sqrt, bias=eps_c[:, 0:1]))
                prev_hs2 = hs2

                def fin(v):
                    v.reciprocal(rstd2[:], sd2[:])
                    return v.tensor_scalar(out_t[:], y2_t[:], negm2[:, 0:1],
                                           rstd2[:, 0:1], ALU.add, ALU.mult)
                hf = S.step("dve", [hs2, prev_stout], fin)
                prev_hf = hf

                def stout(e, dma, ti=ti):
                    dma(out=out_d[ti * 128:(ti + 1) * 128, :], in_=out_t[:])
                h_last_out = S.step("gp", [hf], stout, n_dma=1)
                prev_stout = h_last_out

            tgt = S.cnt["gp"]

            def gp_final(e):
                e.wait_ge(sem["gp"], tgt)
            block.gpsimd(gp_final)

    return nc


_CACHE = {}


def _get_nc(dbg=False):
    key = bool(dbg)
    if key not in _CACHE:
        _CACHE[key] = build_program(dbg=dbg)
    return _CACHE[key]


def _prep_inputs(inputs):
    x = np.asarray(inputs["L0_attn_x"], np.float32)
    Wq = np.asarray(inputs["L0_attn_Wq"], np.float32)
    bq = np.asarray(inputs["L0_attn_bq"], np.float32)
    Wk = np.asarray(inputs["L0_attn_Wk"], np.float32)
    bk = np.asarray(inputs["L0_attn_bk"], np.float32)
    Wv = np.asarray(inputs["L0_attn_Wv"], np.float32)
    bv = np.asarray(inputs["L0_attn_bv"], np.float32)
    Wo = np.asarray(inputs["L0_attn_Wo"], np.float32)
    bo = np.asarray(inputs["L0_attn_bo"], np.float32)
    w1 = np.asarray(inputs["L0_ffn_w1"], np.float32)
    w2 = np.asarray(inputs["L0_ffn_w2"], np.float32)

    shared = {
        "WqT": np.ascontiguousarray(Wq.T), "WkT": np.ascontiguousarray(Wk.T),
        "WvT": np.ascontiguousarray(Wv.T), "WoT": np.ascontiguousarray(Wo.T),
        "bq_t": np.ascontiguousarray(bq.reshape(ND, 128).T),
        "bk_t": np.ascontiguousarray(bk.reshape(ND, 128).T),
        "bv_b": np.ascontiguousarray(np.broadcast_to(bv, (128, D))),
        "bk_b": np.ascontiguousarray(np.broadcast_to(bk, (128, D))),
        "bo_b": np.ascontiguousarray(np.broadcast_to(bo, (128, D))),
        "w1T": np.ascontiguousarray(w1.T), "w2T": np.ascontiguousarray(w2.T),
        "triu": np.triu(np.ones((128, 128), np.float32)),
        "ident": np.eye(128, dtype=np.float32),
    }
    in_maps = []
    for core in range(8):
        b, q = divmod(core, 4)
        seg = x[b, q * TSEG:(q + 1) * TSEG, :]
        m = dict(shared)
        m["x_seg"] = np.ascontiguousarray(seg)
        m["xT_seg"] = np.ascontiguousarray(seg.T)
        pm = (np.arange(4) < q).astype(np.float32)
        m["pmask"] = np.ascontiguousarray(np.broadcast_to(pm, (128, 4)))
        in_maps.append(m)
    return in_maps


def kernel(**inputs):
    in_maps = _prep_inputs(inputs)
    nc = _get_nc(dbg=False)
    res = run_bass_kernel_spmd(nc, in_maps, core_ids=list(range(8)))
    out = np.empty((B, T, D), np.float32)
    for core in range(8):
        b, q = divmod(core, 4)
        out[b, q * TSEG:(q + 1) * TSEG, :] = np.asarray(res.results[core]["out"])
    return out
